# version 23
# speedup vs baseline: 1.8002x; 1.0097x over previous
import math
import contextlib
import numpy as np
import concourse.bass as bass
import concourse.mybir as mybir
from concourse.bass_utils import run_bass_kernel_spmd

F32 = mybir.dt.float32
BF16 = mybir.dt.bfloat16
AF = mybir.ActivationFunctionType
ALU = mybir.AluOpType
AX = mybir.AxisListType

S = 2048
D = 2048
NH = 16
HS = 128
HF = 8192
NT = S // 128
ND = D // 128
NHC = HF // 128

ISQ = 1.0 / math.sqrt(HS)
NSQ = -math.sqrt(HS)
LN_EPS = 1.0e-5
RMS_EPS = 1.1920928955078125e-07
MASK_NEG = -1.0e9

ENGS = ("tensor", "vector", "scalar", "gpsimd", "sync")
DMAQ = ("sync", "scalar", "gpsimd")
SELF_SYNC = ("gpsimd", "vector", "scalar")


class _Op:
    __slots__ = ("eng", "emit", "dma", "seq", "deps", "inc", "dma_val", "inc_val")

    def __init__(self, eng, emit, dma):
        self.eng = eng
        self.emit = emit
        self.dma = dma
        self.deps = []
        self.inc = False
        self.seq = 0


class Builder:

    R = 4

    def __init__(self, nc, ctx):
        self.nc = nc
        self.block = ctx.enter_context(nc.Block())
        self.esem = {e: ctx.enter_context(nc.semaphore("es_" + e)) for e in ENGS}
        self.dsem = {q: [ctx.enter_context(nc.semaphore(f"ds_{q}{l}"))
                         for l in range(self.R)] for q in DMAQ}
        self.etotal = {e: 0 for e in ENGS}
        self.dtotal = {q: 0 for q in DMAQ}
        self.ops = []
        self.lastw = {}
        self.readers = {}
        self.floors = {e: {} for e in ENGS}

    def op(self, eng, emit, reads=(), writes=(), dma=False):
        o = _Op(eng, emit, dma)
        o.seq = len(self.ops)
        deps = []
        for k in reads:
            w = self.lastw.get(k)
            if w is not None:
                deps.append(w)
        for k in writes:
            w = self.lastw.get(k)
            if w is not None:
                deps.append(w)
            deps.extend(self.readers.get(k, ()))
        seen = set()
        for d in deps:
            if d is o or id(d) in seen:
                continue
            seen.add(id(d))
            if d.dma:
                o.deps.append(("d", d))
            elif d.eng != eng or eng in SELF_SYNC:
                o.deps.append(("e", d))
                d.inc = True
        for k in writes:
            self.lastw[k] = o
            self.readers[k] = []
        for k in reads:
            self.readers.setdefault(k, [])
            if o not in self.readers[k]:
                self.readers[k].append(o)
        self.ops.append(o)
        return o

    def dmaop(self, q, emit, reads=(), writes=()):
        return self.op(q, emit, reads, writes, dma=True)

    def _lane(self, q, i):
        lane = i % self.R
        lane_idx = i // self.R
        return lane, (lane_idx + 1) * 16, lane_idx * 16

    def flush(self):
        didx = dict(self.dtotal)
        eidx = dict(self.etotal)
        per_eng = {e: [] for e in ENGS}
        last_compute = {}
        for o in self.ops:
            per_eng[o.eng].append(o)
            if o.dma:
                lane, done, gate = self._lane(o.eng, didx[o.eng])
                didx[o.eng] += 1
                o.dma_val = (lane, done, gate)
            else:
                last_compute[o.eng] = o
        for o in last_compute.values():
            o.inc = True
        for o in self.ops:
            if (not o.dma) and o.inc:
                eidx[o.eng] += 1
                o.inc_val = eidx[o.eng]
        barrier = [(("e", e), self.esem[e], self.etotal[e]) for e in ENGS]
        for q in DMAQ:
            n = self.dtotal[q]
            for l in range(self.R):
                v = ((n - 1 - l) // self.R + 1) * 16 if n > l else 0
                barrier.append((("d", q, l), self.dsem[q][l], v))
        for eng in ENGS:
            ops = per_eng[eng]
            if not ops:
                continue
            fl = self.floors[eng]

            def body(engine, ops=ops, fl=fl, eng=eng):
                first = True
                for o in ops:
                    if first:
                        for sid, sem, val in barrier:
                            if val > 0 and fl.get(sid, 0) < val:
                                engine.wait_ge(sem, val)
                                fl[sid] = val
                        first = False
                    for kind, d in o.deps:
                        if kind == "e":
                            sem, sid, val = self.esem[d.eng], ("e", d.eng), d.inc_val
                        else:
                            lane, done, _ = d.dma_val
                            sem, sid, val = self.dsem[d.eng][lane], ("d", d.eng, lane), done
                        if fl.get(sid, 0) < val:
                            engine.wait_ge(sem, val)
                            fl[sid] = val
                    if o.dma:
                        lane, done, gate = o.dma_val
                        sid = ("d", eng, lane)
                        if gate > 0 and fl.get(sid, 0) < gate:
                            engine.wait_ge(self.dsem[eng][lane], gate)
                            fl[sid] = gate
                        o.emit(engine).then_inc(self.dsem[eng][lane], 16)
                    else:
                        ins = o.emit(engine)
                        if o.inc:
                            ins.then_inc(self.esem[eng], 1)

            getattr(self.block, eng)(body)
        self.etotal = eidx
        self.dtotal = {q: didx[q] for q in DMAQ}
        self.ops = []
        self.lastw = {}
        self.readers = {}

    def finish(self):
        tot = dict(self.dtotal)
        etot = dict(self.etotal)
        fl = self.floors["sync"]

        def body(engine):
            for e in ENGS:
                if etot[e] > 0 and fl.get(("e", e), 0) < etot[e]:
                    engine.wait_ge(self.esem[e], etot[e])
            for q in DMAQ:
                n = tot[q]
                for l in range(self.R):
                    v = ((n - 1 - l) // self.R + 1) * 16 if n > l else 0
                    if v > 0 and fl.get(("d", q, l), 0) < v:
                        engine.wait_ge(self.dsem[q][l], v)

        self.block.sync(body)


def _r3(ap, p=128, inner=None):
    return ap.rearrange("(b p) c -> p b c", p=p)


def build_nc(dump=()):
    nc = bass.Bass("TRN2", target_bir_lowering=False)
    dump = set(dump)

    x_d = nc.dram_tensor("x", [S, D], F32, kind="ExternalInput").ap()
    wqkv_d = nc.dram_tensor("wqkv", [D, 3 * D], F32, kind="ExternalInput").ap()
    bqkvt_d = nc.dram_tensor("bqkvt", [128, 48], F32, kind="ExternalInput").ap()
    bqkvr_d = nc.dram_tensor("bqkvr", [1, 3 * D], F32, kind="ExternalInput").ap()
    wo_d = nc.dram_tensor("wo", [D, D], F32, kind="ExternalInput").ap()
    bo_d = nc.dram_tensor("bo", [1, D], F32, kind="ExternalInput").ap()
    w1t_d = nc.dram_tensor("w1t", [D, 2 * HF], F32, kind="ExternalInput").ap()
    wpt_d = nc.dram_tensor("wpt", [HF, D], F32, kind="ExternalInput").ap()
    y_d = nc.dram_tensor("y", [S, D], F32, kind="ExternalOutput").ap()

    def scratch(name, shape, dt):
        kind = "ExternalOutput" if name in dump else "Internal"
        return nc.dram_tensor(name, shape, dt, kind=kind).ap()

    dbg_t = {}

    attn_d = scratch("s_attn", [D, S], BF16)
    x1_d = scratch("s_x1", [S, D], F32)
    h_d = scratch("s_h", [HF, S], BF16)
    wpq_d = scratch("s_wpq", [HF, D], BF16)
    ffn_d = scratch("s_ffn", [S, D], F32)

    with contextlib.ExitStack() as ctx:
        ident_b = ctx.enter_context(nc.sbuf_tensor("ident_b", [128, 128], BF16))
        ident_f = ctx.enter_context(nc.sbuf_tensor("ident_f", [128, 128], F32))
        ident1 = ctx.enter_context(nc.sbuf_tensor("ident1", [1, 1], F32))
        mask1 = ctx.enter_context(nc.sbuf_tensor("mask1", [128, 128], F32))
        mask2 = ctx.enter_context(nc.sbuf_tensor("mask2", [128, 128], F32))
        ones_blk = ctx.enter_context(nc.sbuf_tensor("ones_blk", [128, 2], BF16))
        ones_bc = ctx.enter_context(nc.sbuf_tensor("ones_bc", [2, 128], BF16))
        ones_col = ctx.enter_context(nc.sbuf_tensor("ones_col", [128, 1], BF16))
        ones_r1b = ctx.enter_context(nc.sbuf_tensor("ones_r1b", [1, 128], BF16))
        ones_r1f = ctx.enter_context(nc.sbuf_tensor("ones_r1f", [1, 128], F32))
        ones2 = ctx.enter_context(nc.sbuf_tensor("ones2", [2, 128], BF16))
        bdiag = ctx.enter_context(nc.sbuf_tensor("bdiag", [128, 128], BF16))
        bqkvt = ctx.enter_context(nc.sbuf_tensor("sb_bqkvt", [128, 48], F32))
        inv_col = ctx.enter_context(nc.sbuf_tensor("inv_col", [128, 16], F32))
        stat = ctx.enter_context(nc.sbuf_tensor("stat", [128, 16], F32))

        b = Builder(nc, ctx)

        def dbg(name, ap, reads):
            if name not in dump:
                return
            shp = list(ap.shape)
            t = nc.dram_tensor("dbg_" + name, shp, ap.dtype,
                               kind="ExternalOutput").ap()
            b.dmaop("scalar", lambda e, t=t, ap=ap: e.dma_start(out=t, in_=ap),
                    reads=tuple(reads), writes=())

        def aff(t, cmp, fill, base, pat, chm):
            return lambda g, t=t, cmp=cmp, fill=fill, base=base, pat=pat, chm=chm: \
                g.affine_select(out=t[:], in_=t[:], compare_op=cmp, fill=fill,
                                base=base, pattern=pat, channel_multiplier=chm)

        for t, key in ((ident_b, "idb"), (ident_f, "idf")):
            b.op("gpsimd", lambda g, t=t: g.memset(t[:], 0.0), writes=(key,))
            b.op("gpsimd", aff(t, ALU.not_equal, 1.0, 0, [[-1, 128]], 1),
                 reads=(key,), writes=(key,))
        b.op("gpsimd", lambda g: g.memset(ident1[:], 1.0), writes=("id1",))
        b.op("gpsimd", lambda g: g.memset(mask1[:], 0.0), writes=("m1",))
        b.op("gpsimd", aff(mask1, ALU.is_ge, MASK_NEG, 0, [[-1, 128]], 1),
             reads=("m1",), writes=("m1",))
        b.op("gpsimd", lambda g: g.memset(mask2[:], 0.0), writes=("m2",))
        b.op("gpsimd", aff(mask2, ALU.is_ge, MASK_NEG, 0, [[1, 128]], -1),
             reads=("m2",), writes=("m2",))
        b.op("gpsimd", lambda g: g.memset(ones_blk[:], 1.0), writes=("ob",))
        b.op("gpsimd", aff(ones_blk, ALU.is_ge, 0.0, 0, [[-64, 2]], 1),
             reads=("ob",), writes=("ob",))
        b.op("gpsimd", aff(ones_blk, ALU.is_ge, 0.0, 63, [[64, 2]], -1),
             reads=("ob",), writes=("ob",))
        b.op("gpsimd", lambda g: g.memset(ones_bc[:], 1.0), writes=("obc",))
        b.op("gpsimd", aff(ones_bc, ALU.is_ge, 0.0, 0, [[1, 128]], -64),
             reads=("obc",), writes=("obc",))
        b.op("gpsimd", aff(ones_bc, ALU.is_ge, 0.0, 63, [[-1, 128]], 64),
             reads=("obc",), writes=("obc",))
        b.op("gpsimd", lambda g: g.memset(ones_col[:], 1.0), writes=("oc",))
        b.op("gpsimd", lambda g: g.memset(ones_r1b[:], 1.0), writes=("o1b",))
        b.op("gpsimd", lambda g: g.memset(ones_r1f[:], 1.0), writes=("o1f",))
        b.op("gpsimd", lambda g: g.memset(ones2[:], 1.0), writes=("o2",))
        b.dmaop("sync", lambda e: e.dma_start(out=bqkvt[:], in_=bqkvt_d),
                writes=("bqt",))
        with nc.psum_tensor("ps_init", [128, 128], F32) as ps_init:
            b.op("tensor", lambda t: t.matmul(ps_init[:], ones_bc[:], ones_bc[:],
                                              start=True, stop=True),
                 reads=("obc",), writes=("psi",))
            b.op("vector", lambda v: v.tensor_scalar(bdiag[:], ps_init[:],
                                                     1.0 / 64.0, None, ALU.mult),
                 reads=("psi",), writes=("bd",))
            b.flush()

        def ln_stats(xap, xkey, scr, scrkey, width, par=0):
            o = par * 8
            def c(i):
                return stat[:, o + i:o + i + 1]
            def k(i):
                return f"st{par}_{i}"
            b.op("vector", lambda v: v.reduce_sum(c(0), xap, axis=AX.X),
                 reads=(xkey,), writes=(k(0),))
            b.op("scalar", lambda s: s.activation(scr, xap, AF.Square,
                                                  accum_out=c(1)),
                 reads=(xkey,), writes=(scrkey, k(1)))
            b.op("vector", lambda v: v.tensor_scalar(c(2), c(0),
                                                     1.0 / width, None, ALU.mult),
                 reads=(k(0),), writes=(k(2),))
            b.op("vector", lambda v: v.tensor_scalar(c(3), c(1),
                                                     1.0 / width, None, ALU.mult),
                 reads=(k(1),), writes=(k(3),))
            b.op("vector", lambda v: v.tensor_tensor(c(4), c(2), c(2), ALU.mult),
                 reads=(k(2),), writes=(k(4),))
            b.op("vector", lambda v: v.tensor_tensor(c(5), c(3), c(4),
                                                     ALU.subtract),
                 reads=(k(3), k(4)), writes=(k(5),))
            b.op("vector", lambda v: v.tensor_scalar(c(5), c(5),
                                                     LN_EPS, None, ALU.add),
                 reads=(k(5),), writes=(k(5),))
            b.op("vector", lambda v: v.reciprocal(c(6), c(5)),
                 reads=(k(5),), writes=(k(6),))
            b.op("scalar", lambda s: s.activation(c(7), c(6), AF.Sqrt),
                 reads=(k(6),), writes=(k(7),))
            return c(2), c(7), (k(2), k(7))

        sA = ctx.enter_context(contextlib.ExitStack())
        ln1fm = sA.enter_context(nc.sbuf_tensor("ln1fm", [128, ND * S], BF16))
        ln1fm_v = ln1fm[:].rearrange("p (b c) -> p b c", c=S)
        with contextlib.ExitStack() as s0:
            xt = [s0.enter_context(nc.sbuf_tensor(f"xt{i}", [128, D], F32))
                  for i in range(2)]
            lnt = [s0.enter_context(nc.sbuf_tensor(f"lnt{i}", [128, D], BF16))
                   for i in range(2)]
            scr = s0.enter_context(nc.sbuf_tensor("scr", [128, D], F32))
            pt = [s0.enter_context(nc.psum_tensor(f"pt{i}", [128, 1024], BF16))
                  for i in range(2)]
            for ti in range(NT):
                xb = xt[ti % 2]
                xk = f"x{ti % 2}"
                b.dmaop("sync", lambda e, ti=ti, xb=xb:
                        e.dma_start(out=xb[:], in_=x_d[ti * 128:(ti + 1) * 128, :]),
                        writes=(xk,))
                m, inv, mk = ln_stats(xb[:], xk, scr[:], "scr", D, par=ti % 2)
                lt, ltk = lnt[ti % 2], f"lnt{ti % 2}"
                b.op("vector", lambda v, xb=xb, m=m, inv=inv, lt=lt:
                     v.tensor_scalar(lt[:], xb[:], m, inv, ALU.subtract, ALU.mult),
                     reads=(xk,) + mk, writes=(ltk,))
                for half in range(2):
                    pb = pt[half]
                    pk = f"pt{half}"
                    for j in range(8):
                        dc = half * 8 + j
                        b.op("tensor", lambda t, dc=dc, pb=pb, j=j, lt=lt:
                             t.transpose(pb[:, j * 128:(j + 1) * 128],
                                         lt[:, dc * 128:(dc + 1) * 128],
                                         ident_b[:]),
                             reads=(ltk, "idb"), writes=(pk,))
                    dst = ln1fm_v[:, half * 8:half * 8 + 8,
                                  ti * 128:(ti + 1) * 128]
                    b.op("vector", lambda v, pb=pb, dst=dst:
                         v.tensor_copy(dst, pb[:].rearrange("p (b c) -> p b c", c=128)),
                         reads=(pk,), writes=("ln1fm",))
            dbg("ln1fm", ln1fm[:], ("ln1fm",))
            b.flush()

        sB = ctx.enter_context(contextlib.ExitStack())
        v_sb = sB.enter_context(nc.sbuf_tensor("v_sb", [128, NT * D], BF16))
        with contextlib.ExitStack() as s1:
            wv = [s1.enter_context(nc.sbuf_tensor(f"wv{i}", [128, ND * 512], BF16))
                  for i in range(2)]
            bqvr = s1.enter_context(nc.sbuf_tensor("bqvr", [1, D], BF16))
            pv = [s1.enter_context(nc.psum_tensor(f"pv{i}", [128, 512], F32))
                  for i in range(2)]
            b.dmaop("gpsimd", lambda e: e.dma_start(out=bqvr[:],
                                                    in_=bqkvr_d[:, 2 * D:3 * D]),
                    writes=("bqr",))
            for vc in range(4):
                wvb = wv[vc % 2]
                wvk = f"wv{vc % 2}"
                src = _r3(wqkv_d[:, 2 * D + vc * 512: 2 * D + (vc + 1) * 512])
                b.dmaop("gpsimd", lambda e, wvb=wvb, src=src:
                        e.dma_start(out=wvb[:].rearrange("p (b c) -> p b c", c=512),
                                    in_=src),
                        writes=(wvk,))
                for ti in range(NT):
                    pb = pv[ti % 2]
                    pk = f"pv{ti % 2}"
                    for dc in range(ND):
                        b.op("tensor", lambda t, dc=dc, ti=ti, pb=pb, wvb=wvb:
                             t.matmul(pb[:],
                                      ln1fm_v[:, dc, ti * 128:(ti + 1) * 128],
                                      wvb[:, dc * 512:(dc + 1) * 512],
                                      start=(dc == 0), stop=False),
                             reads=("ln1fm", wvk), writes=(pk,))
                    b.op("tensor", lambda t, vc=vc, pb=pb:
                         t.matmul(pb[:], ones_r1b[:],
                                  bqvr[:, vc * 512:(vc + 1) * 512],
                                  start=False, stop=True),
                         reads=("o1b", "bqr"), writes=(pk,))
                    b.op("vector", lambda v, ti=ti, vc=vc, pb=pb:
                         v.tensor_copy(v_sb[:, ti * D + vc * 512: ti * D + (vc + 1) * 512],
                                       pb[:]),
                         reads=(pk,), writes=("v_sb",))
            dbg("v_sb", v_sb[:], ("v_sb",))
            b.flush()

        with contextlib.ExitStack() as s2:
            wq = [s2.enter_context(nc.sbuf_tensor(f"wq{i}", [128, ND * 128], BF16))
                  for i in range(2)]
            wk = [s2.enter_context(nc.sbuf_tensor(f"wk{i}", [128, ND * 128], BF16))
                  for i in range(2)]
            q_sb = s2.enter_context(nc.sbuf_tensor("q_sb", [128, S], BF16))
            k_sb = s2.enter_context(nc.sbuf_tensor("k_sb", [128, S], BF16))
            ptb = [s2.enter_context(nc.sbuf_tensor(f"ptb{i}", [128, S], BF16))
                   for i in range(2)]
            e1scr = s2.enter_context(nc.sbuf_tensor("e1scr", [128, S], BF16))
            atn = s2.enter_context(nc.sbuf_tensor("atn", [128, S], BF16))
            cbuf = s2.enter_context(nc.sbuf_tensor("cbuf", [1, 8], F32))
            bias_col = s2.enter_context(nc.sbuf_tensor("bias_col", [128, 1], F32))
            den_sb = s2.enter_context(nc.sbuf_tensor("den_sb", [1, S], F32))
            rden_b = s2.enter_context(nc.sbuf_tensor("rden_b", [1, S], F32))
            bqkvt_s = s2.enter_context(nc.sbuf_tensor("bqkvt_s", [128, 48], F32))
            b.op("vector", lambda v: v.tensor_copy(bqkvt_s[:, 16:48],
                                                   bqkvt[:, 16:48]),
                 reads=("bqt",), writes=("bqt2",))
            b.op("vector", lambda v: v.tensor_scalar(bqkvt_s[:, 0:16],
                                                     bqkvt[:, 0:16], ISQ, None,
                                                     ALU.mult),
                 reads=("bqt",), writes=("bqt2",))
            ps_s = s2.enter_context(nc.psum_tensor("ps_s", [128, S], F32))
            ps_a = s2.enter_context(nc.psum_tensor("ps_a", [128, S], F32))

            def skey(c):
                return ("ps_s", c)

            def akey(c):
                return ("ps_a", c)

            for n in range(NH):
                par = n % 2
                wqb, wkb = wq[par], wk[par]
                wqk, wkk = f"wq{par}", f"wk{par}"
                b.dmaop("gpsimd", lambda e, n=n, wqb=wqb:
                        e.dma_start(out=wqb[:].rearrange("p (b c) -> p b c", c=128),
                                    in_=_r3(wqkv_d[:, n * 128:(n + 1) * 128])),
                        writes=(wqk,))
                b.dmaop("gpsimd", lambda e, n=n, wkb=wkb:
                        e.dma_start(out=wkb[:].rearrange("p (b c) -> p b c", c=128),
                                    in_=_r3(wqkv_d[:, D + n * 128: D + (n + 1) * 128])),
                        writes=(wkk,))
                for (dst, dkey, wb, wkey2, bcol, P, PK) in (
                        (q_sb, "q_sb", wqb, wqk, n, ps_s, skey),
                        (k_sb, "k_sb", wkb, wkk, 16 + n, ps_a, akey)):
                    for tc in range(4):
                        for dc in range(ND):
                            b.op("tensor", lambda t, dc=dc, tc=tc, wb=wb, P=P:
                                 t.matmul(P[:, tc * 512:(tc + 1) * 512],
                                          wb[:, dc * 128:(dc + 1) * 128],
                                          ln1fm_v[:, dc, tc * 512:(tc + 1) * 512],
                                          start=(dc == 0), stop=(dc == ND - 1)),
                                 reads=("ln1fm", wkey2), writes=(PK(tc),))
                    sc = ISQ if dst is q_sb else 1.0
                    b.op("scalar", lambda s, dst=dst, bcol=bcol, P=P, sc=sc:
                         s.activation(dst[:], P[:], AF.Identity,
                                      bias=bqkvt_s[:, bcol:bcol + 1], scale=sc),
                         reads=("bqt2",) + tuple(PK(c) for c in range(4)),
                         writes=(dkey,))
                    sq = e1scr if dst is q_sb else atn
                    sqk = "e1s" if dst is q_sb else "atn"
                    b.op("scalar", lambda s, dst=dst, sq=sq:
                         s.activation(sq[:], dst[:], AF.Square),
                         reads=(dkey,), writes=(sqk,))
                for (sq, sqk, P, PK, mcol) in ((e1scr, "e1s", ps_s, skey, 0),
                                               (atn, "atn", ps_a, akey, 1)):
                    for c in range(4):
                        b.op("tensor", lambda t, c=c, P=P, sq=sq, sqk=sqk:
                             t.matmul(P[0:1, c * 512:(c + 1) * 512], ones_col[:],
                                      sq[:, c * 512:(c + 1) * 512],
                                      start=True, stop=True),
                             reads=("oc", sqk), writes=(PK(c),))
                    b.op("vector", lambda v, P=P, mcol=mcol:
                         v.reduce_max(cbuf[:, mcol:mcol + 1], P[0:1, :],
                                      axis=AX.X),
                         reads=tuple(PK(c) for c in range(4)),
                         writes=(f"cm{mcol}",))
                b.op("vector", lambda v:
                     v.tensor_tensor(cbuf[:, 2:3], cbuf[:, 0:1], cbuf[:, 1:2],
                                     ALU.mult),
                     reads=("cm0", "cm1"), writes=("cm2",))
                b.op("scalar", lambda s:
                     s.activation(cbuf[:, 3:4], cbuf[:, 2:3], AF.Sqrt),
                     reads=("cm2",), writes=("cm3",))
                b.op("tensor", lambda t:
                     t.matmul(ps_a[:, 0:1], ones_r1f[:], cbuf[:, 3:4],
                              start=True, stop=True),
                     reads=("o1f", "cm3"), writes=(akey(0),))
                b.op("scalar", lambda s:
                     s.activation(bias_col[:], ps_a[:, 0:1], AF.Identity,
                                  scale=-1.0),
                     reads=(akey(0),), writes=("bcol",))
                b.op("gpsimd", lambda g: g.memset(den_sb[:], 0.0),
                     writes=("densb",))
                if n == 0:
                    dbg("q0", q_sb[:], ("q_sb",))
                    dbg("k0", k_sb[:], ("k_sb",))
                def spans_of(j):
                    t0 = j * 128
                    sp = [(t0, 512 - (t0 - (j // 4) * 512))]
                    t1 = t0 + sp[0][1]
                    while t1 < S:
                        sp.append((t1, 512))
                        t1 += 512
                    return sp

                def emit_pv(j):
                    pb = ptb[j % 2]
                    pk = f"ptb{j % 2}"
                    for (a, w) in spans_of(j):
                        c = a // 512
                        b.op("tensor", lambda t, j=j, n=n, a=a, w=w, pb=pb:
                             t.matmul(ps_a[:, a:a + w],
                                      v_sb[:, j * D + n * 128: j * D + (n + 1) * 128],
                                      pb[:, a:a + w],
                                      start=(j == 0),
                                      stop=(j == min(15, (a // 512) * 4 + 3))),
                             reads=("v_sb", pk), writes=(akey(c),))

                def emit_den(j):
                    pb = ptb[j % 2]
                    pk = f"ptb{j % 2}"
                    for (a, w) in spans_of(j):
                        c = a // 512
                        b.op("tensor", lambda t, a=a, w=w, pb=pb:
                             t.matmul(ps_s[0:1, a:a + w], ones_col[:],
                                      pb[:, a:a + w], start=True, stop=True),
                             reads=("oc", pk), writes=(skey(c),))
                        b.op("vector", lambda v, a=a, w=w:
                             v.tensor_tensor(den_sb[:, a:a + w],
                                             ps_s[0:1, a:a + w],
                                             den_sb[:, a:a + w], ALU.add),
                             reads=(skey(c), "densb"), writes=("densb",))

                def finish_region(c):
                    b.op("vector", lambda v, c=c:
                         v.reciprocal(rden_b[:, c * 512:(c + 1) * 512],
                                      den_sb[:, c * 512:(c + 1) * 512]),
                         reads=("densb",), writes=("rdb",))
                    b.op("tensor", lambda t, c=c:
                         t.matmul(ps_s[:, c * 512:(c + 1) * 512], ones_r1f[:],
                                  rden_b[:, c * 512:(c + 1) * 512],
                                  start=True, stop=True),
                         reads=("o1f", "rdb"), writes=(skey(c),))
                    b.op("scalar", lambda s, c=c:
                         s.activation(e1scr[:, c * 512:(c + 1) * 512],
                                      ps_s[:, c * 512:(c + 1) * 512],
                                      AF.Identity),
                         reads=(skey(c),), writes=("e1s",))
                    b.op("vector", lambda v, c=c:
                         v.tensor_tensor(atn[:, c * 512:(c + 1) * 512],
                                         ps_a[:, c * 512:(c + 1) * 512],
                                         e1scr[:, c * 512:(c + 1) * 512],
                                         ALU.mult),
                         reads=(akey(c), "e1s"), writes=("atn",))
                    b.dmaop("sync", lambda e, n=n, c=c:
                            e.dma_start(out=attn_d[n * 128:(n + 1) * 128,
                                                   c * 512:(c + 1) * 512],
                                        in_=atn[:, c * 512:(c + 1) * 512]),
                            reads=("atn",), writes=("attn_d",))

                for j in range(NT):
                    pb = ptb[j % 2]
                    pk = f"ptb{j % 2}"
                    t0 = j * 128
                    spans = spans_of(j)
                    if j > 0:
                        emit_den(j - 1)
                        emit_pv(j - 1)
                    for (a, w) in spans:
                        c = a // 512
                        b.op("tensor", lambda t, j=j, a=a, w=w:
                             t.matmul(ps_s[:, a:a + w],
                                      k_sb[:, j * 128:(j + 1) * 128],
                                      q_sb[:, a:a + w],
                                      start=True, stop=True),
                             reads=("q_sb", "k_sb"), writes=(skey(c),))
                    dk = skey(j // 4)
                    b.op("vector", lambda v, j=j:
                         v.tensor_tensor(ps_s[:, j * 128:(j + 1) * 128],
                                         ps_s[:, j * 128:(j + 1) * 128],
                                         mask2[:], ALU.add),
                         reads=(dk, "m2"), writes=(dk,))
                    banks = tuple(skey(c) for c in range(j // 4, 4))
                    b.op("scalar", lambda s, pb=pb, t0=t0:
                         s.activation(pb[:, t0:], ps_s[:, t0:], AF.Exp,
                                      bias=bias_col[:], scale=1.0),
                         reads=banks + ("bcol",), writes=(pk,))
                    if j > 0 and (j - 1) % 4 == 3:
                        finish_region((j - 1) // 4)
                emit_den(NT - 1)
                emit_pv(NT - 1)
                finish_region(3)
                if n == 0:
                    dbg("pt00", ptb[0][:], ("ptb0",))
            b.flush()
        sB.close()
        sA.close()

        sC = ctx.enter_context(contextlib.ExitStack())
        ln3fm = sC.enter_context(nc.sbuf_tensor("ln3fm", [128, ND * S], BF16))
        ln3fm_v = ln3fm[:].rearrange("p (b c) -> p b c", c=S)
        with contextlib.ExitStack() as s3:
            wo_sb = s3.enter_context(nc.sbuf_tensor("wo_sb", [128, ND * D], BF16))
            acb = [s3.enter_context(nc.sbuf_tensor(f"acb{i}", [128, D], BF16))
                   for i in range(2)]
            wo_t = s3.enter_context(nc.sbuf_tensor("wo_t", [128, D], F32))
            xt3 = s3.enter_context(nc.sbuf_tensor("xt3", [128, D], F32))
            x1t = s3.enter_context(nc.sbuf_tensor("x1t", [128, D], F32))
            lnt3 = [s3.enter_context(nc.sbuf_tensor(f"lnt3{i}", [128, D], BF16))
                    for i in range(2)]
            scr3 = s3.enter_context(nc.sbuf_tensor("scr3", [128, D], F32))
            sb_bor = s3.enter_context(nc.sbuf_tensor("sb_bor", [1, D], BF16))
            ps_wo = s3.enter_context(nc.psum_tensor("ps_wo", [128, D], F32))
            b.dmaop("gpsimd", lambda e: e.dma_start(out=sb_bor[:], in_=bo_d),
                    writes=("bor",))
            pt3 = [s3.enter_context(nc.psum_tensor(f"pt3{i}", [128, 1024], BF16))
                   for i in range(2)]
            for g4 in range(4):
                b.dmaop("gpsimd", lambda e, g4=g4:
                        e.dma_start(
                            out=wo_sb[:].rearrange("p (b c) -> p b c", c=D)[:, g4 * 4:(g4 + 1) * 4, :],
                            in_=_r3(wo_d)[:, g4 * 4:(g4 + 1) * 4, :]),
                        writes=(("wo_sb", g4),))
            for ti in range(NT):
                ab = acb[ti % 2]
                ak = f"acb{ti % 2}"
                b.dmaop("sync", lambda e, ti=ti, ab=ab:
                        e.dma_start(out=ab[:].rearrange("p (b c) -> p b c", c=128),
                                    in_=_r3(attn_d[:, ti * 128:(ti + 1) * 128])),
                        reads=("attn_d",), writes=(ak,))
                for ac in range(ND):
                    for dc in range(4):
                        b.op("tensor", lambda t, dc=dc, ac=ac, ab=ab:
                             t.matmul(ps_wo[:, dc * 512:(dc + 1) * 512],
                                      ab[:, ac * 128:(ac + 1) * 128],
                                      wo_sb[:, ac * D + dc * 512: ac * D + (dc + 1) * 512],
                                      start=(ac == 0), stop=False),
                             reads=(ak, ("wo_sb", ac // 4)), writes=(("ps_wo", dc),))
                for dc in range(4):
                    b.op("tensor", lambda t, dc=dc:
                         t.matmul(ps_wo[:, dc * 512:(dc + 1) * 512], ones_r1b[:],
                                  sb_bor[:, dc * 512:(dc + 1) * 512],
                                  start=False, stop=True),
                         reads=("o1b", "bor"), writes=(("ps_wo", dc),))
                b.op("scalar", lambda s: s.activation(wo_t[:], ps_wo[:],
                                                      AF.Identity),
                     reads=tuple(("ps_wo", c) for c in range(4)), writes=("wo_t",))
                m, inv, mk = ln_stats(wo_t[:], "wo_t", scr3[:], "scr3", D, par=ti % 2)
                b.dmaop("sync", lambda e, ti=ti:
                        e.dma_start(out=xt3[:], in_=x_d[ti * 128:(ti + 1) * 128, :]),
                        writes=("xt3",))
                b.op("vector", lambda v, m=m, inv=inv:
                     v.tensor_scalar(wo_t[:], wo_t[:], m, inv, ALU.subtract, ALU.mult),
                     reads=("wo_t",) + mk, writes=("wo_t",))
                b.op("vector", lambda v:
                     v.tensor_tensor(x1t[:], wo_t[:], xt3[:], ALU.add),
                     reads=("wo_t", "xt3"), writes=("x1t",))
                b.dmaop("scalar", lambda e, ti=ti:
                        e.dma_start(out=x1_d[ti * 128:(ti + 1) * 128, :], in_=x1t[:]),
                        reads=("x1t",), writes=("x1_d",))
                m3, inv3, mk3 = ln_stats(x1t[:], "x1t", scr3[:], "scr3", D, par=ti % 2)
                lt3, lt3k = lnt3[ti % 2], f"lnt3{ti % 2}"
                b.op("vector", lambda v, m3=m3, inv3=inv3, lt3=lt3:
                     v.tensor_scalar(lt3[:], x1t[:], m3, inv3, ALU.subtract, ALU.mult),
                     reads=("x1t",) + mk3, writes=(lt3k,))
                for half in range(2):
                    pb = pt3[half]
                    pk = f"pt3{half}"
                    for j in range(8):
                        dc = half * 8 + j
                        b.op("tensor", lambda t, dc=dc, pb=pb, j=j, lt3=lt3:
                             t.transpose(pb[:, j * 128:(j + 1) * 128],
                                         lt3[:, dc * 128:(dc + 1) * 128],
                                         ident_b[:]),
                             reads=(lt3k, "idb"), writes=(pk,))
                    dst = ln3fm_v[:, half * 8:half * 8 + 8, ti * 128:(ti + 1) * 128]
                    b.op("vector", lambda v, pb=pb, dst=dst:
                         v.tensor_copy(dst, pb[:].rearrange("p (b c) -> p b c", c=128)),
                         reads=(pk,), writes=("ln3fm",))
            b.flush()

        with contextlib.ExitStack() as s4:
            wg = [s4.enter_context(nc.sbuf_tensor(f"wg{i}", [128, D], BF16))
                  for i in range(2)]
            wu = [s4.enter_context(nc.sbuf_tensor(f"wu{i}", [128, D], BF16))
                  for i in range(2)]
            wp = [s4.enter_context(nc.sbuf_tensor(f"wp{i}", [128, D], BF16))
                  for i in range(2)]
            wgq = [s4.enter_context(nc.sbuf_tensor(f"wgq{i}", [128, D], BF16))
                   for i in range(2)]
            wuq = [s4.enter_context(nc.sbuf_tensor(f"wuq{i}", [128, D], BF16))
                   for i in range(2)]
            wpq = [s4.enter_context(nc.sbuf_tensor(f"wpq{i}", [128, D], BF16))
                   for i in range(2)]
            aw = [s4.enter_context(nc.sbuf_tensor(f"aw{i}", [128, D], BF16))
                  for i in range(3)]
            s128 = [s4.enter_context(nc.sbuf_tensor(f"s128_{i}", [128, D], BF16))
                    for i in range(3)]
            cp = [s4.enter_context(nc.sbuf_tensor(f"cp{i}", [128, D], BF16))
                  for i in range(3)]
            sg = [s4.enter_context(nc.sbuf_tensor(f"sg{i}", [128, 512], BF16))
                  for i in range(2)]
            h_buf = [s4.enter_context(nc.sbuf_tensor(f"h_buf{i}", [128, S], BF16))
                     for i in range(2)]
            h2b = [s4.enter_context(nc.sbuf_tensor(f"h2b{i}", [128, S], BF16))
                   for i in range(2)]
            ps_g = [s4.enter_context(nc.psum_tensor(f"ps_g{i}", [128, 512], F32))
                    for i in range(2)]
            ps_u = [s4.enter_context(nc.psum_tensor(f"ps_u{i}", [128, 512], F32))
                    for i in range(2)]
            ps_q = s4.enter_context(nc.psum_tensor("ps_q", [1, 512], F32))
            ps_bq = s4.enter_context(nc.psum_tensor("ps_bq", [128, 1536], F32))
            sumh2 = s4.enter_context(nc.sbuf_tensor("sumh2", [1, S], F32))
            invr = s4.enter_context(nc.sbuf_tensor("invr", [1, S], F32))
            b.op("gpsimd", lambda g: g.memset(sumh2[:], 0.0), writes=("sh2",))

            def quant_abs(ty, wt, wkey):
                A, ka = aw[ty], f"aw{ty}"
                b.op("scalar", lambda s, A=A: s.activation(A[:], wt[:], AF.Abs),
                     reads=(wkey,), writes=(ka,))

            def quant(ty, wt, wkey, out, okey):
                A, S1, C = aw[ty], s128[ty], cp[ty]
                ka, ks, kc = f"aw{ty}", f"s128_{ty}", f"cp{ty}"
                for c in range(4):
                    qs = (c % 3) * 512
                    b.op("tensor", lambda t, c=c, qs=qs, A=A:
                         t.matmul(ps_bq[:, qs:qs + 512], bdiag[:],
                                  A[:, c * 512:(c + 1) * 512],
                                  start=True, stop=True),
                         reads=("bd", ka), writes=(("ps_bq", c % 3),))
                    b.op("vector", lambda v, c=c, qs=qs, S1=S1:
                         v.tensor_single_scalar(S1[:, c * 512:(c + 1) * 512],
                                                ps_bq[:, qs:qs + 512],
                                                1e-8, ALU.max),
                         reads=(("ps_bq", c % 3),), writes=(ks,))
                b.op("vector", lambda v, C=C, S1=S1:
                     v.scalar_tensor_tensor(C[:], wt[:], 2.0, S1[:],
                                            ALU.mult, ALU.is_gt),
                     reads=(ks, wkey), writes=(kc,))
                b.op("vector", lambda v, A=A, S1=S1:
                     v.scalar_tensor_tensor(A[:], wt[:], -2.0, S1[:],
                                            ALU.mult, ALU.is_gt),
                     reads=(ks, wkey), writes=(ka,))
                b.op("vector", lambda v, C=C, A=A:
                     v.tensor_tensor(C[:], C[:], A[:], ALU.subtract),
                     reads=(kc, ka), writes=(kc,))
                b.op("gpsimd", lambda g, out=out, C=C, S1=S1:
                     g.tensor_tensor(out[:], C[:], S1[:], ALU.mult),
                     reads=(kc, ks), writes=(okey,))

            def load_stage(g):
                par = g % 2
                wgb, wub, wpb = wg[par], wu[par], wp[par]
                kg, ku, kp = f"wg{par}", f"wu{par}", f"wp{par}"
                b.dmaop("gpsimd", lambda e, g=g, wgb=wgb:
                        e.dma_start(out=wgb[:].rearrange("p (b c) -> p b c", c=128),
                                    in_=_r3(w1t_d[:, g * 128:(g + 1) * 128])),
                        writes=(kg,))
                b.dmaop("gpsimd", lambda e, g=g, wub=wub:
                        e.dma_start(out=wub[:].rearrange("p (b c) -> p b c", c=128),
                                    in_=_r3(w1t_d[:, HF + g * 128: HF + (g + 1) * 128])),
                        writes=(ku,))
                b.dmaop("gpsimd", lambda e, g=g, wpb=wpb:
                        e.dma_start(out=wpb[:],
                                    in_=wpt_d[g * 128:(g + 1) * 128, :]),
                        writes=(kp,))

            def quant_abs_stage(g):
                par = g % 2
                quant_abs(0, wg[par], f"wg{par}")
                quant_abs(1, wu[par], f"wu{par}")
                quant_abs(2, wp[par], f"wp{par}")

            def _wtile(g, ty):
                par = g % 2
                wt = (wg[par], wu[par], wp[par])[ty]
                wkey = (f"wg{par}", f"wu{par}", f"wp{par}")[ty]
                out = (wgq[par], wuq[par], wpq[par])[ty]
                okey = (f"wgq{par}", f"wuq{par}", f"wpq{par}")[ty]
                return wt, wkey, out, okey

            def quant_slot(g, tc):
                for i in range(3):
                    cc = tc * 3 + i
                    ty, c = divmod(cc, 4)
                    bank = cc % 3
                    A, S1 = aw[ty], s128[ty]
                    ka, ks = f"aw{ty}", f"s128_{ty}"
                    b.op("tensor", lambda t, c=c, bank=bank, A=A:
                         t.matmul(ps_bq[:, bank * 512:(bank + 1) * 512],
                                  bdiag[:], A[:, c * 512:(c + 1) * 512],
                                  start=True, stop=True),
                         reads=("bd", ka), writes=(("ps_bq", bank),))
                    b.op("vector", lambda v, c=c, bank=bank, S1=S1:
                         v.tensor_single_scalar(S1[:, c * 512:(c + 1) * 512],
                                                ps_bq[:, bank * 512:(bank + 1) * 512],
                                                1e-8, ALU.max),
                         reads=(("ps_bq", bank),), writes=(ks,))
            def vector_tail(g, ty):
                wt, wkey, _, _ = _wtile(g, ty)
                A, S1, C = aw[ty], s128[ty], cp[ty]
                ka, ks, kc = f"aw{ty}", f"s128_{ty}", f"cp{ty}"
                b.op("vector", lambda v, C=C, S1=S1, wt=wt:
                     v.scalar_tensor_tensor(C[:], wt[:], 2.0, S1[:],
                                            ALU.mult, ALU.is_gt),
                     reads=(ks, wkey), writes=(kc,))
                b.op("vector", lambda v, A=A, S1=S1, wt=wt:
                     v.scalar_tensor_tensor(A[:], wt[:], -2.0, S1[:],
                                            ALU.mult, ALU.is_gt),
                     reads=(ks, wkey), writes=(ka,))
                b.op("vector", lambda v, C=C, A=A:
                     v.tensor_tensor(C[:], C[:], A[:], ALU.subtract),
                     reads=(kc, ka), writes=(kc,))

            def mult_tails(g):
                for ty in range(3):
                    _, _, out, okey = _wtile(g, ty)
                    C, S1 = cp[ty], s128[ty]
                    kc, ks = f"cp{ty}", f"s128_{ty}"
                    b.op("gpsimd", lambda gp, out=out, C=C, S1=S1:
                         gp.tensor_tensor(out[:], C[:], S1[:], ALU.mult),
                         reads=(kc, ks), writes=(okey,))
                    if ty == 2:
                        b.dmaop("sync", lambda e, g=g, out=out:
                                e.dma_start(out=wpq_d[g * 128:(g + 1) * 128, :],
                                            in_=out[:]),
                                reads=(okey,), writes=("wpq_d",))

            def quant_compute(g):
                par = g % 2
                wgb, wub, wpb = wg[par], wu[par], wp[par]
                kg, ku, kp = f"wg{par}", f"wu{par}", f"wp{par}"
                wgqb, wuqb, wpqb = wgq[par], wuq[par], wpq[par]
                kgq, kuq, kpq = f"wgq{par}", f"wuq{par}", f"wpq{par}"
                quant(0, wgb, kg, wgqb, kgq)
                quant(1, wub, ku, wuqb, kuq)
                quant(2, wpb, kp, wpqb, kpq)
                b.dmaop("scalar", lambda e, g=g, wpqb=wpqb:
                        e.dma_start(out=wpq_d[g * 128:(g + 1) * 128, :], in_=wpqb[:]),
                        reads=(kpq,), writes=("wpq_d",))

            def sum_h2_pair(g, tc):
                par = g % 2
                b.op("tensor", lambda t, tc=tc, par=par:
                     t.matmul(ps_q[:], ones_col[:],
                              h2b[par][:, tc * 512:(tc + 1) * 512],
                              start=True, stop=True),
                     reads=("oc", f"h2b{par}"), writes=("ps_q",))
                b.op("vector", lambda v, tc=tc:
                     v.tensor_tensor(sumh2[:, tc * 512:(tc + 1) * 512],
                                     ps_q[:],
                                     sumh2[:, tc * 512:(tc + 1) * 512],
                                     ALU.add),
                     reads=("ps_q", "sh2"), writes=("sh2",))

            def sum_h2_stage(g):
                for tc in range(4):
                    sum_h2_pair(g, tc)

            load_stage(0)
            quant_abs_stage(0)
            quant_compute(0)
            load_stage(1)
            quant_abs_stage(1)
            quant_compute(1)
            load_stage(2)
            for g in range(NHC):
                if g + 2 < NHC:
                    quant_abs_stage(g + 2)
                if g + 3 < NHC:
                    load_stage(g + 3)
                par = g % 2
                wgqb, wuqb, wpqb = wgq[par], wuq[par], wpq[par]
                kgq, kuq, kpq = f"wgq{par}", f"wuq{par}", f"wpq{par}"
                hb = h_buf[par]
                khb = f"h_buf{par}"
                for tc in range(4):
                    if tc >= 2 and g + 2 < NHC:
                        vector_tail(g + 2, tc - 2)
                    pg, pu = ps_g[tc % 2], ps_u[tc % 2]
                    kpg, kpu = f"ps_g{tc % 2}", f"ps_u{tc % 2}"
                    sgt = sg[tc % 2]
                    ksg = f"sg{tc % 2}"
                    for dc in range(ND):
                        b.op("tensor", lambda t, dc=dc, tc=tc, pg=pg, wgqb=wgqb:
                             t.matmul(pg[:],
                                      wgqb[:, dc * 128:(dc + 1) * 128],
                                      ln3fm_v[:, dc, tc * 512:(tc + 1) * 512],
                                      start=(dc == 0), stop=(dc == ND - 1)),
                             reads=(kgq, "ln3fm"), writes=(kpg,))
                        b.op("tensor", lambda t, dc=dc, tc=tc, pu=pu, wuqb=wuqb:
                             t.matmul(pu[:],
                                      wuqb[:, dc * 128:(dc + 1) * 128],
                                      ln3fm_v[:, dc, tc * 512:(tc + 1) * 512],
                                      start=(dc == 0), stop=(dc == ND - 1)),
                             reads=(kuq, "ln3fm"), writes=(kpu,))
                    b.op("scalar", lambda s, pg=pg, sgt=sgt:
                         s.activation(sgt[:], pg[:], AF.Silu),
                         reads=(kpg,), writes=(ksg,))
                    b.op("vector", lambda v, tc=tc, pu=pu, sgt=sgt, hb=hb:
                         v.tensor_tensor(hb[:, tc * 512:(tc + 1) * 512],
                                         sgt[:], pu[:], ALU.mult),
                         reads=(ksg, kpu), writes=(khb,))
                    if g > 0:
                        sum_h2_pair(g - 1, tc)
                    if g + 2 < NHC:
                        quant_slot(g + 2, tc)
                b.op("scalar", lambda s, hb=hb, par=par:
                     s.activation(h2b[par][:], hb[:], AF.Square),
                     reads=(khb,), writes=(f"h2b{par}",))
                if g + 2 < NHC:
                    vector_tail(g + 2, 2)
                    mult_tails(g + 2)
                b.dmaop("sync", lambda e, g=g, hb=hb:
                        e.dma_start(out=h_d[g * 128:(g + 1) * 128, :], in_=hb[:]),
                        reads=(khb,), writes=("h_d",))
            sum_h2_stage(NHC - 1)
            b.op("vector", lambda v:
                 v.tensor_scalar(invr[:], sumh2[:], 1.0 / HF, RMS_EPS,
                                 ALU.mult, ALU.add),
                 reads=("sh2",), writes=("invr",))
            b.op("vector", lambda v: v.reciprocal(invr[:], invr[:]),
                 reads=("invr",), writes=("invr",))
            b.op("scalar", lambda s: s.activation(invr[:], invr[:], AF.Sqrt),
                 reads=("invr",), writes=("invr",))
            for i in range(16):
                b.op("tensor", lambda t, i=i:
                     t.transpose(ps_bq[:, i:i + 1], invr[:, i * 128:(i + 1) * 128],
                                 ident1[:]),
                     reads=("invr", "id1"), writes=(("ps_bq", 0),))
            b.op("vector", lambda v: v.tensor_copy(inv_col[:], ps_bq[:, 0:16]),
                 reads=(("ps_bq", 0),), writes=("invc",))
            b.flush()
        sC.close()

        with contextlib.ExitStack() as s5:
            wph = s5.enter_context(nc.sbuf_tensor("wph", [128, NHC * 1024], BF16))
            hcb = [s5.enter_context(nc.sbuf_tensor(f"hcb{i}", [128, HF], BF16))
                   for i in range(2)]
            ffn_t = [s5.enter_context(nc.sbuf_tensor(f"ffn_t{i}", [128, 1024], F32))
                     for i in range(2)]
            ft = [s5.enter_context(nc.sbuf_tensor(f"ft{i}", [128, 1024], F32))
                  for i in range(2)]
            x1b = [s5.enter_context(nc.sbuf_tensor(f"x1b{i}", [128, D], F32))
                   for i in range(2)]
            scr6 = s5.enter_context(nc.sbuf_tensor("scr6", [128, D], F32))
            ps_f = [s5.enter_context(nc.psum_tensor(f"ps_f{i}", [128, 512], F32))
                    for i in range(4)]

            def out_stage(ti):
                fb, xb = ft[ti % 2], x1b[ti % 2]
                fk, xk = f"ft{ti % 2}", f"x1b{ti % 2}"
                fT, fTk = ffn_t[ti % 2], f"ffn_t{ti % 2}"
                o = (ti % 2) * 8
                def c(i):
                    return stat[:, o + i:o + i + 1]
                def k(i):
                    return f"st{ti % 2}_{i}"
                b.dmaop("sync", lambda e, ti=ti, fb=fb:
                        e.dma_start(out=fb[:],
                                    in_=ffn_d[ti * 128:(ti + 1) * 128, 0:1024]),
                        reads=("ffn_d",), writes=(fk,))
                b.dmaop("gpsimd", lambda e, ti=ti, xb=xb:
                        e.dma_start(out=xb[:], in_=x1_d[ti * 128:(ti + 1) * 128, :]),
                        reads=("x1_d",), writes=(xk,))
                b.op("vector", lambda v, fb=fb: v.reduce_sum(c(0), fb[:], axis=AX.X),
                     reads=(fk,), writes=(k(0),))
                b.op("vector", lambda v, fT=fT: v.reduce_sum(c(3), fT[:], axis=AX.X),
                     reads=(fTk,), writes=(k(3),))
                b.op("vector", lambda v: v.tensor_tensor(c(0), c(0), c(3), ALU.add),
                     reads=(k(0), k(3)), writes=(k(0),))
                b.op("scalar", lambda s, fb=fb:
                     s.activation(scr6[:, 0:1024], fb[:], AF.Square, accum_out=c(1)),
                     reads=(fk,), writes=("scr6", k(1)))
                b.op("scalar", lambda s, fT=fT:
                     s.activation(scr6[:, 1024:2048], fT[:], AF.Square,
                                  accum_out=c(4)),
                     reads=(fTk,), writes=("scr6", k(4)))
                b.op("vector", lambda v: v.tensor_tensor(c(1), c(1), c(4), ALU.add),
                     reads=(k(1), k(4)), writes=(k(1),))
                b.op("vector", lambda v: v.tensor_scalar(c(2), c(0), 1.0 / D, None,
                                                         ALU.mult),
                     reads=(k(0),), writes=(k(2),))
                b.op("vector", lambda v: v.tensor_scalar(c(3), c(1), 1.0 / D, None,
                                                         ALU.mult),
                     reads=(k(1),), writes=(k(3),))
                b.op("vector", lambda v: v.tensor_tensor(c(4), c(2), c(2), ALU.mult),
                     reads=(k(2),), writes=(k(4),))
                b.op("vector", lambda v: v.tensor_tensor(c(5), c(3), c(4),
                                                         ALU.subtract),
                     reads=(k(3), k(4)), writes=(k(5),))
                b.op("vector", lambda v: v.tensor_scalar(c(5), c(5), LN_EPS, None,
                                                         ALU.add),
                     reads=(k(5),), writes=(k(5),))
                b.op("vector", lambda v: v.reciprocal(c(6), c(5)),
                     reads=(k(5),), writes=(k(6),))
                b.op("scalar", lambda s: s.activation(c(7), c(6), AF.Sqrt),
                     reads=(k(6),), writes=(k(7),))
                b.op("vector", lambda v, fb=fb:
                     v.tensor_scalar(scr6[:, 0:1024], fb[:], c(2), c(7),
                                     ALU.subtract, ALU.mult),
                     reads=(fk, k(2), k(7)), writes=("scr6",))
                b.op("vector", lambda v, fT=fT:
                     v.tensor_scalar(scr6[:, 1024:2048], fT[:], c(2), c(7),
                                     ALU.subtract, ALU.mult),
                     reads=(fTk, k(2), k(7)), writes=("scr6",))
                b.op("vector", lambda v, xb=xb:
                     v.tensor_tensor(xb[:], scr6[:], xb[:], ALU.add),
                     reads=("scr6", xk), writes=(xk,))
                b.dmaop("scalar", lambda e, ti=ti, xb=xb:
                        e.dma_start(out=y_d[ti * 128:(ti + 1) * 128, :], in_=xb[:]),
                        reads=(xk,), writes=("y_d",))

            for half in range(2):
                for grp in range(8):
                    b.dmaop("sync", lambda e, half=half, grp=grp:
                            e.dma_start(
                                out=wph[:].rearrange("p (b c) -> p b c", c=1024)[:, grp * 8:(grp + 1) * 8, :],
                                in_=_r3(wpq_d[grp * 1024:(grp + 1) * 1024,
                                              half * 1024:(half + 1) * 1024])),
                            reads=("wpq_d",), writes=(("wph", grp),))
                for ti in range(NT):
                    hb = hcb[ti % 2]
                    hk = f"hcb{ti % 2}"
                    b.dmaop("gpsimd", lambda e, ti=ti, hb=hb:
                            e.dma_start(out=hb[:].rearrange("p (b c) -> p b c", c=128),
                                        in_=_r3(h_d[:, ti * 128:(ti + 1) * 128])),
                            reads=("h_d",), writes=(hk,))
                    for hc in range(NHC):
                        for dc in range(2):
                            pb = ps_f[2 * (ti % 2) + dc]
                            b.op("tensor", lambda t, hc=hc, dc=dc, hb=hb, pb=pb:
                                 t.matmul(pb[:],
                                          hb[:, hc * 128:(hc + 1) * 128],
                                          wph[:, hc * 1024 + dc * 512: hc * 1024 + (dc + 1) * 512],
                                          start=(hc == 0), stop=(hc == NHC - 1)),
                                 reads=(hk, ("wph", hc // 8)),
                                 writes=(f"ps_f{2 * (ti % 2) + dc}",))
                    for dc in range(2):
                        pb = ps_f[2 * (ti % 2) + dc]
                        b.op("vector", lambda v, dc=dc, ti=ti, pb=pb:
                             v.tensor_scalar(ffn_t[ti % 2][:, dc * 512:(dc + 1) * 512],
                                             pb[:], inv_col[:, ti:ti + 1], None,
                                             ALU.mult),
                             reads=(f"ps_f{2 * (ti % 2) + dc}", "invc"),
                             writes=(f"ffn_t{ti % 2}",))
                    if half == 0:
                        b.dmaop("scalar", lambda e, ti=ti:
                                e.dma_start(out=ffn_d[ti * 128:(ti + 1) * 128,
                                                      0:1024],
                                            in_=ffn_t[ti % 2][:]),
                                reads=(f"ffn_t{ti % 2}",), writes=("ffn_d",))
                    else:
                        out_stage(ti)
                b.flush()

        b.finish()
    return nc


def kernel(L0_attn_x, L0_attn_Wqkv, L0_attn_bqkv, L0_attn_Wo, L0_attn_bo,
           L0_ffn_w_gate_up, L0_ffn_w_proj):
    x = np.ascontiguousarray(np.asarray(L0_attn_x, dtype=np.float32))
    wqkv = np.ascontiguousarray(np.asarray(L0_attn_Wqkv, dtype=np.float32))
    bqkv = np.asarray(L0_attn_bqkv, dtype=np.float32)
    wo = np.ascontiguousarray(np.asarray(L0_attn_Wo, dtype=np.float32))
    bo = np.asarray(L0_attn_bo, dtype=np.float32)
    w1t = np.ascontiguousarray(np.asarray(L0_ffn_w_gate_up, dtype=np.float32).T)
    wpt = np.ascontiguousarray(np.asarray(L0_ffn_w_proj, dtype=np.float32).T)
    bqkvt = np.ascontiguousarray(bqkv.reshape(48, 128).T)
    shared = {
        "wqkv": wqkv, "bqkvt": bqkvt, "bqkvr": bqkv.reshape(1, -1),
        "wo": wo, "bo": bo.reshape(1, -1), "w1t": w1t, "wpt": wpt,
    }
    nc = build_nc()
    in_maps = [dict(shared, x=np.ascontiguousarray(x[c])) for c in range(8)]
    res = run_bass_kernel_spmd(nc, in_maps, core_ids=list(range(8)))
    out = np.stack([np.asarray(res.results[c]["y"]) for c in range(8)], axis=0)
    return out


# version 24 (final)
# speedup vs baseline: 1.9186x; 1.0658x over previous
import math
import contextlib
import numpy as np
import concourse.bass as bass
import concourse.mybir as mybir
from concourse.bass_utils import run_bass_kernel_spmd

F32 = mybir.dt.float32
BF16 = mybir.dt.bfloat16
AF = mybir.ActivationFunctionType
ALU = mybir.AluOpType
AX = mybir.AxisListType

S = 2048
D = 2048
NH = 16
HS = 128
HF = 8192
NT = S // 128
ND = D // 128
NHC = HF // 128

ISQ = 1.0 / math.sqrt(HS)
NSQ = -math.sqrt(HS)
LN_EPS = 1.0e-5
RMS_EPS = 1.1920928955078125e-07
MASK_NEG = -1.0e9

ENGS = ("tensor", "vector", "scalar", "gpsimd", "sync")
DMAQ = ("sync", "scalar", "gpsimd")
SELF_SYNC = ("gpsimd", "vector", "scalar")


class _Op:
    __slots__ = ("eng", "emit", "dma", "seq", "deps", "inc", "dma_val", "inc_val")

    def __init__(self, eng, emit, dma):
        self.eng = eng
        self.emit = emit
        self.dma = dma
        self.deps = []
        self.inc = False
        self.seq = 0


class Builder:

    R = 4

    def __init__(self, nc, ctx):
        self.nc = nc
        self.block = ctx.enter_context(nc.Block())
        self.esem = {e: ctx.enter_context(nc.semaphore("es_" + e)) for e in ENGS}
        self.dsem = {q: [ctx.enter_context(nc.semaphore(f"ds_{q}{l}"))
                         for l in range(self.R)] for q in DMAQ}
        self.etotal = {e: 0 for e in ENGS}
        self.dtotal = {q: 0 for q in DMAQ}
        self.ops = []
        self.lastw = {}
        self.readers = {}
        self.floors = {e: {} for e in ENGS}

    def op(self, eng, emit, reads=(), writes=(), dma=False):
        o = _Op(eng, emit, dma)
        o.seq = len(self.ops)
        deps = []
        for k in reads:
            w = self.lastw.get(k)
            if w is not None:
                deps.append(w)
        for k in writes:
            w = self.lastw.get(k)
            if w is not None:
                deps.append(w)
            deps.extend(self.readers.get(k, ()))
        seen = set()
        for d in deps:
            if d is o or id(d) in seen:
                continue
            seen.add(id(d))
            if d.dma:
                o.deps.append(("d", d))
            elif d.eng != eng or eng in SELF_SYNC:
                o.deps.append(("e", d))
                d.inc = True
        for k in writes:
            self.lastw[k] = o
            self.readers[k] = []
        for k in reads:
            self.readers.setdefault(k, [])
            if o not in self.readers[k]:
                self.readers[k].append(o)
        self.ops.append(o)
        return o

    def dmaop(self, q, emit, reads=(), writes=()):
        return self.op(q, emit, reads, writes, dma=True)

    def _lane(self, q, i):
        lane = i % self.R
        lane_idx = i // self.R
        return lane, (lane_idx + 1) * 16, lane_idx * 16

    def flush(self):
        didx = dict(self.dtotal)
        eidx = dict(self.etotal)
        per_eng = {e: [] for e in ENGS}
        last_compute = {}
        for o in self.ops:
            per_eng[o.eng].append(o)
            if o.dma:
                lane, done, gate = self._lane(o.eng, didx[o.eng])
                didx[o.eng] += 1
                o.dma_val = (lane, done, gate)
            else:
                last_compute[o.eng] = o
        for o in last_compute.values():
            o.inc = True
        for o in self.ops:
            if (not o.dma) and o.inc:
                eidx[o.eng] += 1
                o.inc_val = eidx[o.eng]
        barrier = [(("e", e), self.esem[e], self.etotal[e]) for e in ENGS]
        for q in DMAQ:
            n = self.dtotal[q]
            for l in range(self.R):
                v = ((n - 1 - l) // self.R + 1) * 16 if n > l else 0
                barrier.append((("d", q, l), self.dsem[q][l], v))
        for eng in ENGS:
            ops = per_eng[eng]
            if not ops:
                continue
            fl = self.floors[eng]

            def body(engine, ops=ops, fl=fl, eng=eng):
                first = True
                for o in ops:
                    if first:
                        for sid, sem, val in barrier:
                            if val > 0 and fl.get(sid, 0) < val:
                                engine.wait_ge(sem, val)
                                fl[sid] = val
                        first = False
                    for kind, d in o.deps:
                        if kind == "e":
                            sem, sid, val = self.esem[d.eng], ("e", d.eng), d.inc_val
                        else:
                            lane, done, _ = d.dma_val
                            sem, sid, val = self.dsem[d.eng][lane], ("d", d.eng, lane), done
                        if fl.get(sid, 0) < val:
                            engine.wait_ge(sem, val)
                            fl[sid] = val
                    if o.dma:
                        lane, done, gate = o.dma_val
                        sid = ("d", eng, lane)
                        if gate > 0 and fl.get(sid, 0) < gate:
                            engine.wait_ge(self.dsem[eng][lane], gate)
                            fl[sid] = gate
                        o.emit(engine).then_inc(self.dsem[eng][lane], 16)
                    else:
                        ins = o.emit(engine)
                        if o.inc:
                            ins.then_inc(self.esem[eng], 1)

            getattr(self.block, eng)(body)
        self.etotal = eidx
        self.dtotal = {q: didx[q] for q in DMAQ}
        self.ops = []
        self.lastw = {}
        self.readers = {}

    def finish(self):
        tot = dict(self.dtotal)
        etot = dict(self.etotal)
        fl = self.floors["sync"]

        def body(engine):
            for e in ENGS:
                if etot[e] > 0 and fl.get(("e", e), 0) < etot[e]:
                    engine.wait_ge(self.esem[e], etot[e])
            for q in DMAQ:
                n = tot[q]
                for l in range(self.R):
                    v = ((n - 1 - l) // self.R + 1) * 16 if n > l else 0
                    if v > 0 and fl.get(("d", q, l), 0) < v:
                        engine.wait_ge(self.dsem[q][l], v)

        self.block.sync(body)


def _r3(ap, p=128, inner=None):
    return ap.rearrange("(b p) c -> p b c", p=p)


def build_nc(dump=()):
    nc = bass.Bass("TRN2", target_bir_lowering=False)
    dump = set(dump)

    x_d = nc.dram_tensor("x", [S, D], F32, kind="ExternalInput").ap()
    wqkv_d = nc.dram_tensor("wqkv", [D, 3 * D], F32, kind="ExternalInput").ap()
    bqkvt_d = nc.dram_tensor("bqkvt", [128, 48], F32, kind="ExternalInput").ap()
    bqkvr_d = nc.dram_tensor("bqkvr", [1, 3 * D], F32, kind="ExternalInput").ap()
    wo_d = nc.dram_tensor("wo", [D, D], F32, kind="ExternalInput").ap()
    bo_d = nc.dram_tensor("bo", [1, D], F32, kind="ExternalInput").ap()
    w1t_d = nc.dram_tensor("w1t", [D, 2 * HF], F32, kind="ExternalInput").ap()
    wpt_d = nc.dram_tensor("wpt", [HF, D], F32, kind="ExternalInput").ap()
    y_d = nc.dram_tensor("y", [S, D], F32, kind="ExternalOutput").ap()

    def scratch(name, shape, dt):
        kind = "ExternalOutput" if name in dump else "Internal"
        return nc.dram_tensor(name, shape, dt, kind=kind).ap()

    dbg_t = {}

    attn_d = scratch("s_attn", [D, S], BF16)
    x1_d = scratch("s_x1", [S, D], F32)
    h_d = scratch("s_h", [HF, S], BF16)
    wpq_d = scratch("s_wpq", [HF, D], BF16)
    ffn_d = scratch("s_ffn", [S, D], F32)

    with contextlib.ExitStack() as ctx:
        ident_b = ctx.enter_context(nc.sbuf_tensor("ident_b", [128, 128], BF16))
        ident_f = ctx.enter_context(nc.sbuf_tensor("ident_f", [128, 128], F32))
        ident1 = ctx.enter_context(nc.sbuf_tensor("ident1", [1, 1], F32))
        mask1 = ctx.enter_context(nc.sbuf_tensor("mask1", [128, 128], F32))
        mask2 = ctx.enter_context(nc.sbuf_tensor("mask2", [128, 128], F32))
        ones_blk = ctx.enter_context(nc.sbuf_tensor("ones_blk", [128, 2], BF16))
        ones_bc = ctx.enter_context(nc.sbuf_tensor("ones_bc", [2, 128], BF16))
        ones_col = ctx.enter_context(nc.sbuf_tensor("ones_col", [128, 1], BF16))
        ones_r1b = ctx.enter_context(nc.sbuf_tensor("ones_r1b", [1, 128], BF16))
        ones_r1f = ctx.enter_context(nc.sbuf_tensor("ones_r1f", [1, 128], F32))
        ones2 = ctx.enter_context(nc.sbuf_tensor("ones2", [2, 128], BF16))
        bdiag = ctx.enter_context(nc.sbuf_tensor("bdiag", [128, 128], BF16))
        bqkvt = ctx.enter_context(nc.sbuf_tensor("sb_bqkvt", [128, 48], F32))
        inv_col = ctx.enter_context(nc.sbuf_tensor("inv_col", [128, 16], F32))
        stat = ctx.enter_context(nc.sbuf_tensor("stat", [128, 16], F32))

        b = Builder(nc, ctx)

        def dbg(name, ap, reads):
            if name not in dump:
                return
            shp = list(ap.shape)
            t = nc.dram_tensor("dbg_" + name, shp, ap.dtype,
                               kind="ExternalOutput").ap()
            b.dmaop("scalar", lambda e, t=t, ap=ap: e.dma_start(out=t, in_=ap),
                    reads=tuple(reads), writes=())

        def aff(t, cmp, fill, base, pat, chm):
            return lambda g, t=t, cmp=cmp, fill=fill, base=base, pat=pat, chm=chm: \
                g.affine_select(out=t[:], in_=t[:], compare_op=cmp, fill=fill,
                                base=base, pattern=pat, channel_multiplier=chm)

        for t, key in ((ident_b, "idb"), (ident_f, "idf")):
            b.op("gpsimd", lambda g, t=t: g.memset(t[:], 0.0), writes=(key,))
            b.op("gpsimd", aff(t, ALU.not_equal, 1.0, 0, [[-1, 128]], 1),
                 reads=(key,), writes=(key,))
        b.op("gpsimd", lambda g: g.memset(ident1[:], 1.0), writes=("id1",))
        b.op("gpsimd", lambda g: g.memset(mask1[:], 0.0), writes=("m1",))
        b.op("gpsimd", aff(mask1, ALU.is_ge, MASK_NEG, 0, [[-1, 128]], 1),
             reads=("m1",), writes=("m1",))
        b.op("gpsimd", lambda g: g.memset(mask2[:], 0.0), writes=("m2",))
        b.op("gpsimd", aff(mask2, ALU.is_ge, MASK_NEG, 0, [[1, 128]], -1),
             reads=("m2",), writes=("m2",))
        b.op("gpsimd", lambda g: g.memset(ones_blk[:], 1.0), writes=("ob",))
        b.op("gpsimd", aff(ones_blk, ALU.is_ge, 0.0, 0, [[-64, 2]], 1),
             reads=("ob",), writes=("ob",))
        b.op("gpsimd", aff(ones_blk, ALU.is_ge, 0.0, 63, [[64, 2]], -1),
             reads=("ob",), writes=("ob",))
        b.op("gpsimd", lambda g: g.memset(ones_bc[:], 1.0), writes=("obc",))
        b.op("gpsimd", aff(ones_bc, ALU.is_ge, 0.0, 0, [[1, 128]], -64),
             reads=("obc",), writes=("obc",))
        b.op("gpsimd", aff(ones_bc, ALU.is_ge, 0.0, 63, [[-1, 128]], 64),
             reads=("obc",), writes=("obc",))
        b.op("gpsimd", lambda g: g.memset(ones_col[:], 1.0), writes=("oc",))
        b.op("gpsimd", lambda g: g.memset(ones_r1b[:], 1.0), writes=("o1b",))
        b.op("gpsimd", lambda g: g.memset(ones_r1f[:], 1.0), writes=("o1f",))
        b.op("gpsimd", lambda g: g.memset(ones2[:], 1.0), writes=("o2",))
        b.dmaop("sync", lambda e: e.dma_start(out=bqkvt[:], in_=bqkvt_d),
                writes=("bqt",))
        with nc.psum_tensor("ps_init", [128, 128], F32) as ps_init:
            b.op("tensor", lambda t: t.matmul(ps_init[:], ones_bc[:], ones_bc[:],
                                              start=True, stop=True),
                 reads=("obc",), writes=("psi",))
            b.op("vector", lambda v: v.tensor_scalar(bdiag[:], ps_init[:],
                                                     1.0 / 64.0, None, ALU.mult),
                 reads=("psi",), writes=("bd",))
            b.flush()

        def ln_stats(xap, xkey, scr, scrkey, width, par=0):
            o = par * 8
            def c(i):
                return stat[:, o + i:o + i + 1]
            def k(i):
                return f"st{par}_{i}"
            b.op("vector", lambda v: v.reduce_sum(c(0), xap, axis=AX.X),
                 reads=(xkey,), writes=(k(0),))
            b.op("scalar", lambda s: s.activation(scr, xap, AF.Square,
                                                  accum_out=c(1)),
                 reads=(xkey,), writes=(scrkey, k(1)))
            b.op("vector", lambda v: v.tensor_scalar(c(2), c(0),
                                                     1.0 / width, None, ALU.mult),
                 reads=(k(0),), writes=(k(2),))
            b.op("vector", lambda v: v.tensor_scalar(c(3), c(1),
                                                     1.0 / width, None, ALU.mult),
                 reads=(k(1),), writes=(k(3),))
            b.op("vector", lambda v: v.tensor_tensor(c(4), c(2), c(2), ALU.mult),
                 reads=(k(2),), writes=(k(4),))
            b.op("vector", lambda v: v.tensor_tensor(c(5), c(3), c(4),
                                                     ALU.subtract),
                 reads=(k(3), k(4)), writes=(k(5),))
            b.op("vector", lambda v: v.tensor_scalar(c(5), c(5),
                                                     LN_EPS, None, ALU.add),
                 reads=(k(5),), writes=(k(5),))
            b.op("vector", lambda v: v.reciprocal(c(6), c(5)),
                 reads=(k(5),), writes=(k(6),))
            b.op("scalar", lambda s: s.activation(c(7), c(6), AF.Sqrt),
                 reads=(k(6),), writes=(k(7),))
            return c(2), c(7), (k(2), k(7))

        sA = ctx.enter_context(contextlib.ExitStack())
        ln1fm = sA.enter_context(nc.sbuf_tensor("ln1fm", [128, ND * S], BF16))
        ln1fm_v = ln1fm[:].rearrange("p (b c) -> p b c", c=S)
        with contextlib.ExitStack() as s0:
            xt = [s0.enter_context(nc.sbuf_tensor(f"xt{i}", [128, D], F32))
                  for i in range(2)]
            lnt = [s0.enter_context(nc.sbuf_tensor(f"lnt{i}", [128, D], BF16))
                   for i in range(2)]
            scr = s0.enter_context(nc.sbuf_tensor("scr", [128, D], F32))
            pt = [s0.enter_context(nc.psum_tensor(f"pt{i}", [128, 1024], BF16))
                  for i in range(2)]
            for ti in range(NT):
                xb = xt[ti % 2]
                xk = f"x{ti % 2}"
                b.dmaop("sync", lambda e, ti=ti, xb=xb:
                        e.dma_start(out=xb[:], in_=x_d[ti * 128:(ti + 1) * 128, :]),
                        writes=(xk,))
                m, inv, mk = ln_stats(xb[:], xk, scr[:], "scr", D, par=ti % 2)
                lt, ltk = lnt[ti % 2], f"lnt{ti % 2}"
                b.op("vector", lambda v, xb=xb, m=m, inv=inv, lt=lt:
                     v.tensor_scalar(lt[:], xb[:], m, inv, ALU.subtract, ALU.mult),
                     reads=(xk,) + mk, writes=(ltk,))
                for half in range(2):
                    pb = pt[half]
                    pk = f"pt{half}"
                    for j in range(8):
                        dc = half * 8 + j
                        b.op("tensor", lambda t, dc=dc, pb=pb, j=j, lt=lt:
                             t.transpose(pb[:, j * 128:(j + 1) * 128],
                                         lt[:, dc * 128:(dc + 1) * 128],
                                         ident_b[:]),
                             reads=(ltk, "idb"), writes=(pk,))
                    dst = ln1fm_v[:, half * 8:half * 8 + 8,
                                  ti * 128:(ti + 1) * 128]
                    b.op("vector", lambda v, pb=pb, dst=dst:
                         v.tensor_copy(dst, pb[:].rearrange("p (b c) -> p b c", c=128)),
                         reads=(pk,), writes=("ln1fm",))
            dbg("ln1fm", ln1fm[:], ("ln1fm",))
            b.flush()

        sB = ctx.enter_context(contextlib.ExitStack())
        v_sb = sB.enter_context(nc.sbuf_tensor("v_sb", [128, NT * D], BF16))
        with contextlib.ExitStack() as s1:
            wv = [s1.enter_context(nc.sbuf_tensor(f"wv{i}", [128, ND * 512], BF16))
                  for i in range(2)]
            bqvr = s1.enter_context(nc.sbuf_tensor("bqvr", [1, D], BF16))
            pv = [s1.enter_context(nc.psum_tensor(f"pv{i}", [128, 512], F32))
                  for i in range(2)]
            b.dmaop("gpsimd", lambda e: e.dma_start(out=bqvr[:],
                                                    in_=bqkvr_d[:, 2 * D:3 * D]),
                    writes=("bqr",))
            for vc in range(4):
                wvb = wv[vc % 2]
                wvk = f"wv{vc % 2}"
                src = _r3(wqkv_d[:, 2 * D + vc * 512: 2 * D + (vc + 1) * 512])
                b.dmaop("gpsimd", lambda e, wvb=wvb, src=src:
                        e.dma_start(out=wvb[:].rearrange("p (b c) -> p b c", c=512),
                                    in_=src),
                        writes=(wvk,))
                for ti in range(NT):
                    pb = pv[ti % 2]
                    pk = f"pv{ti % 2}"
                    for dc in range(ND):
                        b.op("tensor", lambda t, dc=dc, ti=ti, pb=pb, wvb=wvb:
                             t.matmul(pb[:],
                                      ln1fm_v[:, dc, ti * 128:(ti + 1) * 128],
                                      wvb[:, dc * 512:(dc + 1) * 512],
                                      start=(dc == 0), stop=False),
                             reads=("ln1fm", wvk), writes=(pk,))
                    b.op("tensor", lambda t, vc=vc, pb=pb:
                         t.matmul(pb[:], ones_r1b[:],
                                  bqvr[:, vc * 512:(vc + 1) * 512],
                                  start=False, stop=True),
                         reads=("o1b", "bqr"), writes=(pk,))
                    b.op("vector", lambda v, ti=ti, vc=vc, pb=pb:
                         v.tensor_copy(v_sb[:, ti * D + vc * 512: ti * D + (vc + 1) * 512],
                                       pb[:]),
                         reads=(pk,), writes=("v_sb",))
            dbg("v_sb", v_sb[:], ("v_sb",))
            b.flush()

        with contextlib.ExitStack() as s2:
            wq = [s2.enter_context(nc.sbuf_tensor(f"wq{i}", [128, ND * 128], BF16))
                  for i in range(2)]
            wk = [s2.enter_context(nc.sbuf_tensor(f"wk{i}", [128, ND * 128], BF16))
                  for i in range(2)]
            q_sb = s2.enter_context(nc.sbuf_tensor("q_sb", [128, S], BF16))
            k_sb = s2.enter_context(nc.sbuf_tensor("k_sb", [128, S], BF16))
            ptb = [s2.enter_context(nc.sbuf_tensor(f"ptb{i}", [128, S], BF16))
                   for i in range(2)]
            e1scr = s2.enter_context(nc.sbuf_tensor("e1scr", [128, S], BF16))
            atn = s2.enter_context(nc.sbuf_tensor("atn", [128, S], BF16))
            cbuf = s2.enter_context(nc.sbuf_tensor("cbuf", [1, 8], F32))
            bias_col = s2.enter_context(nc.sbuf_tensor("bias_col", [128, 1], F32))
            den_sb = s2.enter_context(nc.sbuf_tensor("den_sb", [1, S], F32))
            rden_b = s2.enter_context(nc.sbuf_tensor("rden_b", [1, S], F32))
            bqkvt_s = s2.enter_context(nc.sbuf_tensor("bqkvt_s", [128, 48], F32))
            b.op("vector", lambda v: v.tensor_copy(bqkvt_s[:, 16:48],
                                                   bqkvt[:, 16:48]),
                 reads=("bqt",), writes=("bqt2",))
            b.op("vector", lambda v: v.tensor_scalar(bqkvt_s[:, 0:16],
                                                     bqkvt[:, 0:16], ISQ, None,
                                                     ALU.mult),
                 reads=("bqt",), writes=("bqt2",))
            ps_s = s2.enter_context(nc.psum_tensor("ps_s", [128, S], F32))
            ps_a = s2.enter_context(nc.psum_tensor("ps_a", [128, S], F32))

            def skey(c):
                return ("ps_s", c)

            def akey(c):
                return ("ps_a", c)

            for n in range(NH):
                par = n % 2
                wqb, wkb = wq[par], wk[par]
                wqk, wkk = f"wq{par}", f"wk{par}"
                b.dmaop("gpsimd", lambda e, n=n, wqb=wqb:
                        e.dma_start(out=wqb[:].rearrange("p (b c) -> p b c", c=128),
                                    in_=_r3(wqkv_d[:, n * 128:(n + 1) * 128])),
                        writes=(wqk,))
                b.dmaop("gpsimd", lambda e, n=n, wkb=wkb:
                        e.dma_start(out=wkb[:].rearrange("p (b c) -> p b c", c=128),
                                    in_=_r3(wqkv_d[:, D + n * 128: D + (n + 1) * 128])),
                        writes=(wkk,))
                for (dst, dkey, wb, wkey2, bcol, P, PK) in (
                        (q_sb, "q_sb", wqb, wqk, n, ps_s, skey),
                        (k_sb, "k_sb", wkb, wkk, 16 + n, ps_a, akey)):
                    for tc in range(4):
                        for dc in range(ND):
                            b.op("tensor", lambda t, dc=dc, tc=tc, wb=wb, P=P:
                                 t.matmul(P[:, tc * 512:(tc + 1) * 512],
                                          wb[:, dc * 128:(dc + 1) * 128],
                                          ln1fm_v[:, dc, tc * 512:(tc + 1) * 512],
                                          start=(dc == 0), stop=(dc == ND - 1)),
                                 reads=("ln1fm", wkey2), writes=(PK(tc),))
                    sc = ISQ if dst is q_sb else 1.0
                    b.op("scalar", lambda s, dst=dst, bcol=bcol, P=P, sc=sc:
                         s.activation(dst[:], P[:], AF.Identity,
                                      bias=bqkvt_s[:, bcol:bcol + 1], scale=sc),
                         reads=("bqt2",) + tuple(PK(c) for c in range(4)),
                         writes=(dkey,))
                    sq = e1scr if dst is q_sb else atn
                    sqk = "e1s" if dst is q_sb else "atn"
                    b.op("scalar", lambda s, dst=dst, sq=sq:
                         s.activation(sq[:], dst[:], AF.Square),
                         reads=(dkey,), writes=(sqk,))
                for (sq, sqk, P, PK, mcol) in ((e1scr, "e1s", ps_s, skey, 0),
                                               (atn, "atn", ps_a, akey, 1)):
                    for c in range(4):
                        b.op("tensor", lambda t, c=c, P=P, sq=sq, sqk=sqk:
                             t.matmul(P[0:1, c * 512:(c + 1) * 512], ones_col[:],
                                      sq[:, c * 512:(c + 1) * 512],
                                      start=True, stop=True),
                             reads=("oc", sqk), writes=(PK(c),))
                    b.op("vector", lambda v, P=P, mcol=mcol:
                         v.reduce_max(cbuf[:, mcol:mcol + 1], P[0:1, :],
                                      axis=AX.X),
                         reads=tuple(PK(c) for c in range(4)),
                         writes=(f"cm{mcol}",))
                b.op("vector", lambda v:
                     v.tensor_tensor(cbuf[:, 2:3], cbuf[:, 0:1], cbuf[:, 1:2],
                                     ALU.mult),
                     reads=("cm0", "cm1"), writes=("cm2",))
                b.op("scalar", lambda s:
                     s.activation(cbuf[:, 3:4], cbuf[:, 2:3], AF.Sqrt),
                     reads=("cm2",), writes=("cm3",))
                b.op("tensor", lambda t:
                     t.matmul(ps_a[:, 0:1], ones_r1f[:], cbuf[:, 3:4],
                              start=True, stop=True),
                     reads=("o1f", "cm3"), writes=(akey(0),))
                b.op("scalar", lambda s:
                     s.activation(bias_col[:], ps_a[:, 0:1], AF.Identity,
                                  scale=-1.0),
                     reads=(akey(0),), writes=("bcol",))
                b.op("gpsimd", lambda g: g.memset(den_sb[:], 0.0),
                     writes=("densb",))
                if n == 0:
                    dbg("q0", q_sb[:], ("q_sb",))
                    dbg("k0", k_sb[:], ("k_sb",))
                def spans_of(j):
                    t0 = j * 128
                    sp = [(t0, 512 - (t0 - (j // 4) * 512))]
                    t1 = t0 + sp[0][1]
                    while t1 < S:
                        sp.append((t1, 512))
                        t1 += 512
                    return sp

                def emit_pv(j):
                    pb = ptb[j % 2]
                    pk = f"ptb{j % 2}"
                    for (a, w) in spans_of(j):
                        c = a // 512
                        b.op("tensor", lambda t, j=j, n=n, a=a, w=w, pb=pb:
                             t.matmul(ps_a[:, a:a + w],
                                      v_sb[:, j * D + n * 128: j * D + (n + 1) * 128],
                                      pb[:, a:a + w],
                                      start=(j == 0),
                                      stop=(j == min(15, (a // 512) * 4 + 3))),
                             reads=("v_sb", pk), writes=(akey(c),))

                def emit_den(j):
                    pb = ptb[j % 2]
                    pk = f"ptb{j % 2}"
                    for (a, w) in spans_of(j):
                        c = a // 512
                        b.op("tensor", lambda t, a=a, w=w, pb=pb:
                             t.matmul(ps_s[0:1, a:a + w], ones_col[:],
                                      pb[:, a:a + w], start=True, stop=True),
                             reads=("oc", pk), writes=(skey(c),))
                        b.op("vector", lambda v, a=a, w=w:
                             v.tensor_tensor(den_sb[:, a:a + w],
                                             ps_s[0:1, a:a + w],
                                             den_sb[:, a:a + w], ALU.add),
                             reads=(skey(c), "densb"), writes=("densb",))

                def finish_region(c):
                    b.op("vector", lambda v, c=c:
                         v.reciprocal(rden_b[:, c * 512:(c + 1) * 512],
                                      den_sb[:, c * 512:(c + 1) * 512]),
                         reads=("densb",), writes=("rdb",))
                    b.op("tensor", lambda t, c=c:
                         t.matmul(ps_s[:, c * 512:(c + 1) * 512], ones_r1f[:],
                                  rden_b[:, c * 512:(c + 1) * 512],
                                  start=True, stop=True),
                         reads=("o1f", "rdb"), writes=(skey(c),))
                    b.op("scalar", lambda s, c=c:
                         s.activation(e1scr[:, c * 512:(c + 1) * 512],
                                      ps_s[:, c * 512:(c + 1) * 512],
                                      AF.Identity),
                         reads=(skey(c),), writes=("e1s",))
                    b.op("vector", lambda v, c=c:
                         v.tensor_tensor(atn[:, c * 512:(c + 1) * 512],
                                         ps_a[:, c * 512:(c + 1) * 512],
                                         e1scr[:, c * 512:(c + 1) * 512],
                                         ALU.mult),
                         reads=(akey(c), "e1s"), writes=("atn",))
                    b.dmaop("sync", lambda e, n=n, c=c:
                            e.dma_start(out=attn_d[n * 128:(n + 1) * 128,
                                                   c * 512:(c + 1) * 512],
                                        in_=atn[:, c * 512:(c + 1) * 512]),
                            reads=("atn",), writes=("attn_d",))

                for j in range(NT):
                    pb = ptb[j % 2]
                    pk = f"ptb{j % 2}"
                    t0 = j * 128
                    spans = spans_of(j)
                    if j > 0:
                        emit_den(j - 1)
                        emit_pv(j - 1)
                    for (a, w) in spans:
                        c = a // 512
                        b.op("tensor", lambda t, j=j, a=a, w=w:
                             t.matmul(ps_s[:, a:a + w],
                                      k_sb[:, j * 128:(j + 1) * 128],
                                      q_sb[:, a:a + w],
                                      start=True, stop=True),
                             reads=("q_sb", "k_sb"), writes=(skey(c),))
                    dk = skey(j // 4)
                    b.op("vector", lambda v, j=j:
                         v.tensor_tensor(ps_s[:, j * 128:(j + 1) * 128],
                                         ps_s[:, j * 128:(j + 1) * 128],
                                         mask2[:], ALU.add),
                         reads=(dk, "m2"), writes=(dk,))
                    banks = tuple(skey(c) for c in range(j // 4, 4))
                    b.op("scalar", lambda s, pb=pb, t0=t0:
                         s.activation(pb[:, t0:], ps_s[:, t0:], AF.Exp,
                                      bias=bias_col[:], scale=1.0),
                         reads=banks + ("bcol",), writes=(pk,))
                    if j > 0 and (j - 1) % 4 == 3:
                        finish_region((j - 1) // 4)
                emit_den(NT - 1)
                emit_pv(NT - 1)
                finish_region(3)
                if n == 0:
                    dbg("pt00", ptb[0][:], ("ptb0",))
            b.flush()
        sB.close()
        sA.close()

        sC = ctx.enter_context(contextlib.ExitStack())
        ln3fm = sC.enter_context(nc.sbuf_tensor("ln3fm", [128, ND * S], BF16))
        ln3fm_v = ln3fm[:].rearrange("p (b c) -> p b c", c=S)
        with contextlib.ExitStack() as s3:
            wo_sb = s3.enter_context(nc.sbuf_tensor("wo_sb", [128, ND * D], BF16))
            acb = [s3.enter_context(nc.sbuf_tensor(f"acb{i}", [128, D], BF16))
                   for i in range(2)]
            wo_t = [s3.enter_context(nc.sbuf_tensor(f"wo_t{i}", [128, D], F32))
                    for i in range(2)]
            xt3 = s3.enter_context(nc.sbuf_tensor("xt3", [128, D], F32))
            x1t = s3.enter_context(nc.sbuf_tensor("x1t", [128, D], F32))
            lnt3 = [s3.enter_context(nc.sbuf_tensor(f"lnt3{i}", [128, D], BF16))
                    for i in range(2)]
            scr3 = s3.enter_context(nc.sbuf_tensor("scr3", [128, D], F32))
            sb_bor = s3.enter_context(nc.sbuf_tensor("sb_bor", [1, D], BF16))
            ps_wo = s3.enter_context(nc.psum_tensor("ps_wo", [128, D], F32))
            b.dmaop("gpsimd", lambda e: e.dma_start(out=sb_bor[:], in_=bo_d),
                    writes=("bor",))
            pt3 = [s3.enter_context(nc.psum_tensor(f"pt3{i}", [128, 1024], BF16))
                   for i in range(2)]
            for g4 in range(4):
                b.dmaop("gpsimd", lambda e, g4=g4:
                        e.dma_start(
                            out=wo_sb[:].rearrange("p (b c) -> p b c", c=D)[:, g4 * 4:(g4 + 1) * 4, :],
                            in_=_r3(wo_d)[:, g4 * 4:(g4 + 1) * 4, :]),
                        writes=(("wo_sb", g4),))
            for ti in range(NT):
                ab = acb[ti % 2]
                ak = f"acb{ti % 2}"
                b.dmaop("sync", lambda e, ti=ti, ab=ab:
                        e.dma_start(out=ab[:].rearrange("p (b c) -> p b c", c=128),
                                    in_=_r3(attn_d[:, ti * 128:(ti + 1) * 128])),
                        reads=("attn_d",), writes=(ak,))
                for ac in range(ND):
                    for dc in range(4):
                        b.op("tensor", lambda t, dc=dc, ac=ac, ab=ab:
                             t.matmul(ps_wo[:, dc * 512:(dc + 1) * 512],
                                      ab[:, ac * 128:(ac + 1) * 128],
                                      wo_sb[:, ac * D + dc * 512: ac * D + (dc + 1) * 512],
                                      start=(ac == 0), stop=False),
                             reads=(ak, ("wo_sb", ac // 4)), writes=(("ps_wo", dc),))
                for dc in range(4):
                    b.op("tensor", lambda t, dc=dc:
                         t.matmul(ps_wo[:, dc * 512:(dc + 1) * 512], ones_r1b[:],
                                  sb_bor[:, dc * 512:(dc + 1) * 512],
                                  start=False, stop=True),
                         reads=("o1b", "bor"), writes=(("ps_wo", dc),))
                def emit_tr3(ti):
                    lt3, lt3k = lnt3[ti % 2], f"lnt3{ti % 2}"
                    for half in range(2):
                        pb = pt3[half]
                        pk = f"pt3{half}"
                        for j in range(8):
                            dc = half * 8 + j
                            b.op("tensor", lambda t, dc=dc, pb=pb, j=j, lt3=lt3:
                                 t.transpose(pb[:, j * 128:(j + 1) * 128],
                                             lt3[:, dc * 128:(dc + 1) * 128],
                                             ident_b[:]),
                                 reads=(lt3k, "idb"), writes=(pk,))
                        dst = ln3fm_v[:, half * 8:half * 8 + 8,
                                      ti * 128:(ti + 1) * 128]
                        b.op("vector", lambda v, pb=pb, dst=dst:
                             v.tensor_copy(dst,
                                           pb[:].rearrange("p (b c) -> p b c", c=128)),
                             reads=(pk,), writes=("ln3fm",))

                if ti > 0:
                    emit_tr3(ti - 1)
                wt3, wt3k = wo_t[ti % 2], f"wo_t{ti % 2}"
                b.op("scalar", lambda s, wt3=wt3: s.activation(wt3[:], ps_wo[:],
                                                               AF.Identity),
                     reads=tuple(("ps_wo", c) for c in range(4)), writes=(wt3k,))
                m, inv, mk = ln_stats(wt3[:], wt3k, scr3[:], "scr3", D, par=ti % 2)
                b.dmaop("sync", lambda e, ti=ti:
                        e.dma_start(out=xt3[:], in_=x_d[ti * 128:(ti + 1) * 128, :]),
                        writes=("xt3",))
                b.op("vector", lambda v, m=m, inv=inv, wt3=wt3:
                     v.tensor_scalar(wt3[:], wt3[:], m, inv, ALU.subtract, ALU.mult),
                     reads=(wt3k,) + mk, writes=(wt3k,))
                b.op("vector", lambda v, wt3=wt3:
                     v.tensor_tensor(x1t[:], wt3[:], xt3[:], ALU.add),
                     reads=(wt3k, "xt3"), writes=("x1t",))
                b.dmaop("scalar", lambda e, ti=ti:
                        e.dma_start(out=x1_d[ti * 128:(ti + 1) * 128, :], in_=x1t[:]),
                        reads=("x1t",), writes=("x1_d",))
                m3, inv3, mk3 = ln_stats(x1t[:], "x1t", scr3[:], "scr3", D, par=ti % 2)
                lt3, lt3k = lnt3[ti % 2], f"lnt3{ti % 2}"
                b.op("vector", lambda v, m3=m3, inv3=inv3, lt3=lt3:
                     v.tensor_scalar(lt3[:], x1t[:], m3, inv3, ALU.subtract, ALU.mult),
                     reads=("x1t",) + mk3, writes=(lt3k,))
            emit_tr3(NT - 1)
            b.flush()

        with contextlib.ExitStack() as s4:
            wg = [s4.enter_context(nc.sbuf_tensor(f"wg{i}", [128, D], BF16))
                  for i in range(2)]
            wu = [s4.enter_context(nc.sbuf_tensor(f"wu{i}", [128, D], BF16))
                  for i in range(2)]
            wp = [s4.enter_context(nc.sbuf_tensor(f"wp{i}", [128, D], BF16))
                  for i in range(2)]
            wgq = [s4.enter_context(nc.sbuf_tensor(f"wgq{i}", [128, D], BF16))
                   for i in range(2)]
            wuq = [s4.enter_context(nc.sbuf_tensor(f"wuq{i}", [128, D], BF16))
                   for i in range(2)]
            wpq = [s4.enter_context(nc.sbuf_tensor(f"wpq{i}", [128, D], BF16))
                   for i in range(2)]
            aw = [s4.enter_context(nc.sbuf_tensor(f"aw{i}", [128, D], BF16))
                  for i in range(3)]
            s128 = [s4.enter_context(nc.sbuf_tensor(f"s128_{i}", [128, D], BF16))
                    for i in range(3)]
            cp = [s4.enter_context(nc.sbuf_tensor(f"cp{i}", [128, D], BF16))
                  for i in range(3)]
            sg = [s4.enter_context(nc.sbuf_tensor(f"sg{i}", [128, 512], BF16))
                  for i in range(2)]
            h_buf = [s4.enter_context(nc.sbuf_tensor(f"h_buf{i}", [128, S], BF16))
                     for i in range(2)]
            h2b = [s4.enter_context(nc.sbuf_tensor(f"h2b{i}", [128, S], BF16))
                   for i in range(2)]
            ps_g = [s4.enter_context(nc.psum_tensor(f"ps_g{i}", [128, 512], F32))
                    for i in range(2)]
            ps_u = [s4.enter_context(nc.psum_tensor(f"ps_u{i}", [128, 512], F32))
                    for i in range(2)]
            ps_q = s4.enter_context(nc.psum_tensor("ps_q", [1, 512], F32))
            ps_bq = s4.enter_context(nc.psum_tensor("ps_bq", [128, 1536], F32))
            sumh2 = s4.enter_context(nc.sbuf_tensor("sumh2", [1, S], F32))
            invr = s4.enter_context(nc.sbuf_tensor("invr", [1, S], F32))
            b.op("gpsimd", lambda g: g.memset(sumh2[:], 0.0), writes=("sh2",))

            def quant_abs(ty, wt, wkey):
                A, ka = aw[ty], f"aw{ty}"
                b.op("scalar", lambda s, A=A: s.activation(A[:], wt[:], AF.Abs),
                     reads=(wkey,), writes=(ka,))

            def quant(ty, wt, wkey, out, okey):
                A, S1, C = aw[ty], s128[ty], cp[ty]
                ka, ks, kc = f"aw{ty}", f"s128_{ty}", f"cp{ty}"
                for c in range(4):
                    qs = (c % 3) * 512
                    b.op("tensor", lambda t, c=c, qs=qs, A=A:
                         t.matmul(ps_bq[:, qs:qs + 512], bdiag[:],
                                  A[:, c * 512:(c + 1) * 512],
                                  start=True, stop=True),
                         reads=("bd", ka), writes=(("ps_bq", c % 3),))
                    b.op("vector", lambda v, c=c, qs=qs, S1=S1:
                         v.tensor_single_scalar(S1[:, c * 512:(c + 1) * 512],
                                                ps_bq[:, qs:qs + 512],
                                                1e-8, ALU.max),
                         reads=(("ps_bq", c % 3),), writes=(ks,))
                b.op("vector", lambda v, C=C, S1=S1:
                     v.scalar_tensor_tensor(C[:], wt[:], 2.0, S1[:],
                                            ALU.mult, ALU.is_gt),
                     reads=(ks, wkey), writes=(kc,))
                b.op("vector", lambda v, A=A, S1=S1:
                     v.scalar_tensor_tensor(A[:], wt[:], -2.0, S1[:],
                                            ALU.mult, ALU.is_gt),
                     reads=(ks, wkey), writes=(ka,))
                b.op("vector", lambda v, C=C, A=A:
                     v.tensor_tensor(C[:], C[:], A[:], ALU.subtract),
                     reads=(kc, ka), writes=(kc,))
                b.op("gpsimd", lambda g, out=out, C=C, S1=S1:
                     g.tensor_tensor(out[:], C[:], S1[:], ALU.mult),
                     reads=(kc, ks), writes=(okey,))

            def load_stage(g):
                par = g % 2
                wgb, wub, wpb = wg[par], wu[par], wp[par]
                kg, ku, kp = f"wg{par}", f"wu{par}", f"wp{par}"
                b.dmaop("gpsimd", lambda e, g=g, wgb=wgb:
                        e.dma_start(out=wgb[:].rearrange("p (b c) -> p b c", c=128),
                                    in_=_r3(w1t_d[:, g * 128:(g + 1) * 128])),
                        writes=(kg,))
                b.dmaop("gpsimd", lambda e, g=g, wub=wub:
                        e.dma_start(out=wub[:].rearrange("p (b c) -> p b c", c=128),
                                    in_=_r3(w1t_d[:, HF + g * 128: HF + (g + 1) * 128])),
                        writes=(ku,))
                b.dmaop("gpsimd", lambda e, g=g, wpb=wpb:
                        e.dma_start(out=wpb[:],
                                    in_=wpt_d[g * 128:(g + 1) * 128, :]),
                        writes=(kp,))

            def quant_abs_stage(g):
                par = g % 2
                quant_abs(0, wg[par], f"wg{par}")
                quant_abs(1, wu[par], f"wu{par}")
                quant_abs(2, wp[par], f"wp{par}")

            def _wtile(g, ty):
                par = g % 2
                wt = (wg[par], wu[par], wp[par])[ty]
                wkey = (f"wg{par}", f"wu{par}", f"wp{par}")[ty]
                out = (wgq[par], wuq[par], wpq[par])[ty]
                okey = (f"wgq{par}", f"wuq{par}", f"wpq{par}")[ty]
                return wt, wkey, out, okey

            def quant_slot(g, tc):
                for i in range(3):
                    cc = tc * 3 + i
                    ty, c = divmod(cc, 4)
                    bank = cc % 3
                    A, S1 = aw[ty], s128[ty]
                    ka, ks = f"aw{ty}", f"s128_{ty}"
                    b.op("tensor", lambda t, c=c, bank=bank, A=A:
                         t.matmul(ps_bq[:, bank * 512:(bank + 1) * 512],
                                  bdiag[:], A[:, c * 512:(c + 1) * 512],
                                  start=True, stop=True),
                         reads=("bd", ka), writes=(("ps_bq", bank),))
                    b.op("vector", lambda v, c=c, bank=bank, S1=S1:
                         v.tensor_single_scalar(S1[:, c * 512:(c + 1) * 512],
                                                ps_bq[:, bank * 512:(bank + 1) * 512],
                                                1e-8, ALU.max),
                         reads=(("ps_bq", bank),), writes=(ks,))
            def vector_tail(g, ty):
                wt, wkey, _, _ = _wtile(g, ty)
                A, S1, C = aw[ty], s128[ty], cp[ty]
                ka, ks, kc = f"aw{ty}", f"s128_{ty}", f"cp{ty}"
                b.op("vector", lambda v, C=C, S1=S1, wt=wt:
                     v.scalar_tensor_tensor(C[:], wt[:], 2.0, S1[:],
                                            ALU.mult, ALU.is_gt),
                     reads=(ks, wkey), writes=(kc,))
                b.op("vector", lambda v, A=A, S1=S1, wt=wt:
                     v.scalar_tensor_tensor(A[:], wt[:], -2.0, S1[:],
                                            ALU.mult, ALU.is_gt),
                     reads=(ks, wkey), writes=(ka,))
                b.op("vector", lambda v, C=C, A=A:
                     v.tensor_tensor(C[:], C[:], A[:], ALU.subtract),
                     reads=(kc, ka), writes=(kc,))

            def mult_tails(g):
                for ty in range(3):
                    _, _, out, okey = _wtile(g, ty)
                    C, S1 = cp[ty], s128[ty]
                    kc, ks = f"cp{ty}", f"s128_{ty}"
                    b.op("gpsimd", lambda gp, out=out, C=C, S1=S1:
                         gp.tensor_tensor(out[:], C[:], S1[:], ALU.mult),
                         reads=(kc, ks), writes=(okey,))
                    if ty == 2:
                        b.dmaop("sync", lambda e, g=g, out=out:
                                e.dma_start(out=wpq_d[g * 128:(g + 1) * 128, :],
                                            in_=out[:]),
                                reads=(okey,), writes=("wpq_d",))

            def quant_compute(g):
                par = g % 2
                wgb, wub, wpb = wg[par], wu[par], wp[par]
                kg, ku, kp = f"wg{par}", f"wu{par}", f"wp{par}"
                wgqb, wuqb, wpqb = wgq[par], wuq[par], wpq[par]
                kgq, kuq, kpq = f"wgq{par}", f"wuq{par}", f"wpq{par}"
                quant(0, wgb, kg, wgqb, kgq)
                quant(1, wub, ku, wuqb, kuq)
                quant(2, wpb, kp, wpqb, kpq)
                b.dmaop("scalar", lambda e, g=g, wpqb=wpqb:
                        e.dma_start(out=wpq_d[g * 128:(g + 1) * 128, :], in_=wpqb[:]),
                        reads=(kpq,), writes=("wpq_d",))

            def sum_h2_pair(g, tc):
                par = g % 2
                b.op("tensor", lambda t, tc=tc, par=par:
                     t.matmul(ps_q[:], ones_col[:],
                              h2b[par][:, tc * 512:(tc + 1) * 512],
                              start=True, stop=True),
                     reads=("oc", f"h2b{par}"), writes=("ps_q",))
                b.op("vector", lambda v, tc=tc:
                     v.tensor_tensor(sumh2[:, tc * 512:(tc + 1) * 512],
                                     ps_q[:],
                                     sumh2[:, tc * 512:(tc + 1) * 512],
                                     ALU.add),
                     reads=("ps_q", "sh2"), writes=("sh2",))

            def sum_h2_stage(g):
                for tc in range(4):
                    sum_h2_pair(g, tc)

            load_stage(0)
            quant_abs_stage(0)
            quant_compute(0)
            load_stage(1)
            quant_abs_stage(1)
            quant_compute(1)
            load_stage(2)
            for g in range(NHC):
                if g + 2 < NHC:
                    quant_abs_stage(g + 2)
                if g + 3 < NHC:
                    load_stage(g + 3)
                par = g % 2
                wgqb, wuqb, wpqb = wgq[par], wuq[par], wpq[par]
                kgq, kuq, kpq = f"wgq{par}", f"wuq{par}", f"wpq{par}"
                hb = h_buf[par]
                khb = f"h_buf{par}"
                for tc in range(4):
                    if tc >= 2 and g + 2 < NHC:
                        vector_tail(g + 2, tc - 2)
                    pg, pu = ps_g[tc % 2], ps_u[tc % 2]
                    kpg, kpu = f"ps_g{tc % 2}", f"ps_u{tc % 2}"
                    sgt = sg[tc % 2]
                    ksg = f"sg{tc % 2}"
                    for dc in range(ND):
                        b.op("tensor", lambda t, dc=dc, tc=tc, pg=pg, wgqb=wgqb:
                             t.matmul(pg[:],
                                      wgqb[:, dc * 128:(dc + 1) * 128],
                                      ln3fm_v[:, dc, tc * 512:(tc + 1) * 512],
                                      start=(dc == 0), stop=(dc == ND - 1)),
                             reads=(kgq, "ln3fm"), writes=(kpg,))
                        b.op("tensor", lambda t, dc=dc, tc=tc, pu=pu, wuqb=wuqb:
                             t.matmul(pu[:],
                                      wuqb[:, dc * 128:(dc + 1) * 128],
                                      ln3fm_v[:, dc, tc * 512:(tc + 1) * 512],
                                      start=(dc == 0), stop=(dc == ND - 1)),
                             reads=(kuq, "ln3fm"), writes=(kpu,))
                    b.op("scalar", lambda s, pg=pg, sgt=sgt:
                         s.activation(sgt[:], pg[:], AF.Silu),
                         reads=(kpg,), writes=(ksg,))
                    b.op("vector", lambda v, tc=tc, pu=pu, sgt=sgt, hb=hb:
                         v.tensor_tensor(hb[:, tc * 512:(tc + 1) * 512],
                                         sgt[:], pu[:], ALU.mult),
                         reads=(ksg, kpu), writes=(khb,))
                    if g > 0:
                        sum_h2_pair(g - 1, tc)
                    if g + 2 < NHC:
                        quant_slot(g + 2, tc)
                b.op("scalar", lambda s, hb=hb, par=par:
                     s.activation(h2b[par][:], hb[:], AF.Square),
                     reads=(khb,), writes=(f"h2b{par}",))
                if g + 2 < NHC:
                    vector_tail(g + 2, 2)
                    mult_tails(g + 2)
                b.dmaop("sync", lambda e, g=g, hb=hb:
                        e.dma_start(out=h_d[g * 128:(g + 1) * 128, :], in_=hb[:]),
                        reads=(khb,), writes=("h_d",))
            sum_h2_stage(NHC - 1)
            b.op("vector", lambda v:
                 v.tensor_scalar(invr[:], sumh2[:], 1.0 / HF, RMS_EPS,
                                 ALU.mult, ALU.add),
                 reads=("sh2",), writes=("invr",))
            b.op("vector", lambda v: v.reciprocal(invr[:], invr[:]),
                 reads=("invr",), writes=("invr",))
            b.op("scalar", lambda s: s.activation(invr[:], invr[:], AF.Sqrt),
                 reads=("invr",), writes=("invr",))
            for i in range(16):
                b.op("tensor", lambda t, i=i:
                     t.transpose(ps_bq[:, i:i + 1], invr[:, i * 128:(i + 1) * 128],
                                 ident1[:]),
                     reads=("invr", "id1"), writes=(("ps_bq", 0),))
            b.op("vector", lambda v: v.tensor_copy(inv_col[:], ps_bq[:, 0:16]),
                 reads=(("ps_bq", 0),), writes=("invc",))
            b.flush()
        sC.close()

        with contextlib.ExitStack() as s5:
            wph = s5.enter_context(nc.sbuf_tensor("wph", [128, NHC * 1024], BF16))
            hcb = [s5.enter_context(nc.sbuf_tensor(f"hcb{i}", [128, HF], BF16))
                   for i in range(2)]
            ffn_t = [s5.enter_context(nc.sbuf_tensor(f"ffn_t{i}", [128, 1024], F32))
                     for i in range(2)]
            ft = [s5.enter_context(nc.sbuf_tensor(f"ft{i}", [128, 1024], F32))
                  for i in range(2)]
            x1b = [s5.enter_context(nc.sbuf_tensor(f"x1b{i}", [128, D], F32))
                   for i in range(2)]
            scr6 = s5.enter_context(nc.sbuf_tensor("scr6", [128, D], F32))
            ps_f = [s5.enter_context(nc.psum_tensor(f"ps_f{i}", [128, 512], F32))
                    for i in range(4)]

            def out_stage(ti):
                fb, xb = ft[ti % 2], x1b[ti % 2]
                fk, xk = f"ft{ti % 2}", f"x1b{ti % 2}"
                fT, fTk = ffn_t[ti % 2], f"ffn_t{ti % 2}"
                o = (ti % 2) * 8
                def c(i):
                    return stat[:, o + i:o + i + 1]
                def k(i):
                    return f"st{ti % 2}_{i}"
                b.dmaop("sync", lambda e, ti=ti, fb=fb:
                        e.dma_start(out=fb[:],
                                    in_=ffn_d[ti * 128:(ti + 1) * 128, 0:1024]),
                        reads=("ffn_d",), writes=(fk,))
                b.dmaop("gpsimd", lambda e, ti=ti, xb=xb:
                        e.dma_start(out=xb[:], in_=x1_d[ti * 128:(ti + 1) * 128, :]),
                        reads=("x1_d",), writes=(xk,))
                b.op("vector", lambda v, fb=fb: v.reduce_sum(c(0), fb[:], axis=AX.X),
                     reads=(fk,), writes=(k(0),))
                b.op("vector", lambda v, fT=fT: v.reduce_sum(c(3), fT[:], axis=AX.X),
                     reads=(fTk,), writes=(k(3),))
                b.op("vector", lambda v: v.tensor_tensor(c(0), c(0), c(3), ALU.add),
                     reads=(k(0), k(3)), writes=(k(0),))
                b.op("scalar", lambda s, fb=fb:
                     s.activation(scr6[:, 0:1024], fb[:], AF.Square, accum_out=c(1)),
                     reads=(fk,), writes=("scr6", k(1)))
                b.op("scalar", lambda s, fT=fT:
                     s.activation(scr6[:, 1024:2048], fT[:], AF.Square,
                                  accum_out=c(4)),
                     reads=(fTk,), writes=("scr6", k(4)))
                b.op("vector", lambda v: v.tensor_tensor(c(1), c(1), c(4), ALU.add),
                     reads=(k(1), k(4)), writes=(k(1),))
                b.op("vector", lambda v: v.tensor_scalar(c(2), c(0), 1.0 / D, None,
                                                         ALU.mult),
                     reads=(k(0),), writes=(k(2),))
                b.op("vector", lambda v: v.tensor_scalar(c(3), c(1), 1.0 / D, None,
                                                         ALU.mult),
                     reads=(k(1),), writes=(k(3),))
                b.op("vector", lambda v: v.tensor_tensor(c(4), c(2), c(2), ALU.mult),
                     reads=(k(2),), writes=(k(4),))
                b.op("vector", lambda v: v.tensor_tensor(c(5), c(3), c(4),
                                                         ALU.subtract),
                     reads=(k(3), k(4)), writes=(k(5),))
                b.op("vector", lambda v: v.tensor_scalar(c(5), c(5), LN_EPS, None,
                                                         ALU.add),
                     reads=(k(5),), writes=(k(5),))
                b.op("vector", lambda v: v.reciprocal(c(6), c(5)),
                     reads=(k(5),), writes=(k(6),))
                b.op("scalar", lambda s: s.activation(c(7), c(6), AF.Sqrt),
                     reads=(k(6),), writes=(k(7),))
                b.op("vector", lambda v, fb=fb:
                     v.tensor_scalar(scr6[:, 0:1024], fb[:], c(2), c(7),
                                     ALU.subtract, ALU.mult),
                     reads=(fk, k(2), k(7)), writes=("scr6",))
                b.op("vector", lambda v, fT=fT:
                     v.tensor_scalar(scr6[:, 1024:2048], fT[:], c(2), c(7),
                                     ALU.subtract, ALU.mult),
                     reads=(fTk, k(2), k(7)), writes=("scr6",))
                b.op("vector", lambda v, xb=xb:
                     v.tensor_tensor(xb[:], scr6[:], xb[:], ALU.add),
                     reads=("scr6", xk), writes=(xk,))
                b.dmaop("scalar", lambda e, ti=ti, xb=xb:
                        e.dma_start(out=y_d[ti * 128:(ti + 1) * 128, :], in_=xb[:]),
                        reads=(xk,), writes=("y_d",))

            for half in range(2):
                for grp in range(8):
                    b.dmaop("sync", lambda e, half=half, grp=grp:
                            e.dma_start(
                                out=wph[:].rearrange("p (b c) -> p b c", c=1024)[:, grp * 8:(grp + 1) * 8, :],
                                in_=_r3(wpq_d[grp * 1024:(grp + 1) * 1024,
                                              half * 1024:(half + 1) * 1024])),
                            reads=("wpq_d",), writes=(("wph", grp),))
                for ti in range(NT):
                    hb = hcb[ti % 2]
                    hk = f"hcb{ti % 2}"
                    b.dmaop("gpsimd", lambda e, ti=ti, hb=hb:
                            e.dma_start(out=hb[:].rearrange("p (b c) -> p b c", c=128),
                                        in_=_r3(h_d[:, ti * 128:(ti + 1) * 128])),
                            reads=("h_d",), writes=(hk,))
                    for hc in range(NHC):
                        for dc in range(2):
                            pb = ps_f[2 * (ti % 2) + dc]
                            b.op("tensor", lambda t, hc=hc, dc=dc, hb=hb, pb=pb:
                                 t.matmul(pb[:],
                                          hb[:, hc * 128:(hc + 1) * 128],
                                          wph[:, hc * 1024 + dc * 512: hc * 1024 + (dc + 1) * 512],
                                          start=(hc == 0), stop=(hc == NHC - 1)),
                                 reads=(hk, ("wph", hc // 8)),
                                 writes=(f"ps_f{2 * (ti % 2) + dc}",))
                    for dc in range(2):
                        pb = ps_f[2 * (ti % 2) + dc]
                        b.op("vector", lambda v, dc=dc, ti=ti, pb=pb:
                             v.tensor_scalar(ffn_t[ti % 2][:, dc * 512:(dc + 1) * 512],
                                             pb[:], inv_col[:, ti:ti + 1], None,
                                             ALU.mult),
                             reads=(f"ps_f{2 * (ti % 2) + dc}", "invc"),
                             writes=(f"ffn_t{ti % 2}",))
                    if half == 0:
                        b.dmaop("scalar", lambda e, ti=ti:
                                e.dma_start(out=ffn_d[ti * 128:(ti + 1) * 128,
                                                      0:1024],
                                            in_=ffn_t[ti % 2][:]),
                                reads=(f"ffn_t{ti % 2}",), writes=("ffn_d",))
                    else:
                        out_stage(ti)
                b.flush()

        b.finish()
    return nc


def kernel(L0_attn_x, L0_attn_Wqkv, L0_attn_bqkv, L0_attn_Wo, L0_attn_bo,
           L0_ffn_w_gate_up, L0_ffn_w_proj):
    x = np.ascontiguousarray(np.asarray(L0_attn_x, dtype=np.float32))
    wqkv = np.ascontiguousarray(np.asarray(L0_attn_Wqkv, dtype=np.float32))
    bqkv = np.asarray(L0_attn_bqkv, dtype=np.float32)
    wo = np.ascontiguousarray(np.asarray(L0_attn_Wo, dtype=np.float32))
    bo = np.asarray(L0_attn_bo, dtype=np.float32)
    w1t = np.ascontiguousarray(np.asarray(L0_ffn_w_gate_up, dtype=np.float32).T)
    wpt = np.ascontiguousarray(np.asarray(L0_ffn_w_proj, dtype=np.float32).T)
    bqkvt = np.ascontiguousarray(bqkv.reshape(48, 128).T)
    shared = {
        "wqkv": wqkv, "bqkvt": bqkvt, "bqkvr": bqkv.reshape(1, -1),
        "wo": wo, "bo": bo.reshape(1, -1), "w1t": w1t, "wpt": wpt,
    }
    nc = build_nc()
    in_maps = [dict(shared, x=np.ascontiguousarray(x[c])) for c in range(8)]
    res = run_bass_kernel_spmd(nc, in_maps, core_ids=list(range(8)))
    out = np.stack([np.asarray(res.results[c]["y"]) for c in range(8)], axis=0)
    return out
